# version 04 (final)
# speedup vs baseline: 1.1423x; 1.0174x over previous
import numpy as np
import ml_dtypes
from contextlib import ExitStack
import concourse.bass as bass
import concourse.mybir as mybir
from concourse.bass import IndirectOffsetOnAxis
from concourse.bass_utils import run_bass_kernel_spmd

F32 = mybir.dt.float32
BF16 = mybir.dt.bfloat16
U32 = mybir.dt.uint32
I16 = mybir.dt.int16
U8 = mybir.dt.uint8
AF = mybir.ActivationFunctionType
ALU = mybir.AluOpType
AX = mybir.AxisListType.X

NCORES = 8
B, T, D = 8, 4096, 1024
NTOK = B * T
NT = NTOK // 128
TILES_PER_CORE = T // 128
E0, CAP, TOPK = 32, 1024, 8
NSLOT = E0 * CAP


def _ap_name(x):
    try:
        return x.tensor.name
    except AttributeError:
        return None


class GuardQ:

    SIG = {
        "tensor_scalar": (("out", "accum_out"), ("in0", "scalar1", "scalar2")),
        "tensor_single_scalar": (("out",), ("in_", "scalar")),
        "tensor_tensor": (("out",), ("in0", "in1")),
        "tensor_add": (("out",), ("in0", "in1")),
        "tensor_sub": (("out",), ("in0", "in1")),
        "tensor_mul": (("out",), ("in0", "in1")),
        "tensor_max": (("out",), ("in0", "in1")),
        "tensor_copy": (("out",), ("in_",)),
        "scalar_tensor_tensor": (("out", "accum_out"), ("in0", "scalar", "in1")),
        "reduce_sum": (("out",), ("in_",)),
        "reduce_max": (("out",), ("in_",)),
        "tensor_reduce": (("out",), ("in_",)),
        "max": (("out",), ("in_",)),
        "max_index": (("out",), ("in_max", "in_values")),
        "match_replace": (("out", "in_to_replace"), ("in_values",)),
        "reciprocal": (("out",), ("in_",)),
        "select": (("out",), ("mask", "on_true", "on_false")),
        "memset": (("ap",), ()),
        "memzero": (("ap",), ()),
        "activation": (("out", "accum_out"), ("in_", "bias", "scale")),
        "mul": (("out",), ("in_", "mul")),
        "add": (("out",), ("in_", "add")),
        "copy": (("out",), ("in_",)),
        "square": (("out",), ("in_",)),
        "matmul": (("out",), ("lhsT", "rhs")),
        "transpose": (("out",), ("in_", "identity")),
    }
    POS = {
        "memset": ("ap", "constant"),
        "memzero": ("ap",),
        "reciprocal": ("out", "in_"),
        "max": ("out", "in_"),
        "matmul": ("out", "lhsT", "rhs"),
        "transpose": ("out", "in_", "identity"),
    }
    PE_OPS = ("matmul", "transpose", "ldweights")

    def __init__(self, q, eng, chain_sem):
        self._q = q
        self._eng = eng
        self._chain = chain_sem
        self._log = []
        self._known = -1

    def note_inc(self, inst, sem_h, val):
        for i in range(len(self._log) - 1, -1, -1):
            if self._log[i][2] is inst:
                self._log[i] = (self._log[i][0], self._log[i][1], inst, (sem_h, val))
                return

    def _ensure_edge(self, upto):
        if upto <= self._known:
            return
        w, r, inst, inc = self._log[upto]
        if inc is None:
            assert self._chain[0] is not None, f"no chain sem for {self._eng}"
            self._chain[1] += 1
            inst.then_inc(self._chain[0], 1)
            inc = (self._chain[0], self._chain[1])
            self._log[upto] = (w, r, inst, inc)
        self._q.wait_ge(inc[0], inc[1])
        self._known = upto

    def _guard(self, writes, reads):
        latest = -1
        for i in range(len(self._log) - 1, self._known, -1):
            w, r, _, _ = self._log[i]
            if (reads & w) or (writes & w) or (writes & r):
                latest = i
                break
        if latest >= 0:
            self._ensure_edge(latest)

    def raw(self):
        return self._q

    def wait_ge(self, h, v):
        self._q.wait_ge(h, v)

    def __getattr__(self, name):
        attr = getattr(self._q, name)
        sig = self.SIG.get(name)
        if sig is None:
            return attr
        wk, rk = sig
        is_pe = name in self.PE_OPS

        def wrapped(*a, **k):
            if a:
                names = self.POS.get(name)
                assert names is not None and len(a) <= len(names), \
                    f"{name}: positional args not mapped in GuardQ"
                for v, nm in zip(a, names):
                    k[nm] = v
            writes = {n for key in wk for n in [_ap_name(k.get(key))] if n}
            reads = {n for key in rk for n in [_ap_name(k.get(key))] if n}
            if not is_pe:
                self._guard(writes, reads)
            inst = attr(**k)
            self._log.append((writes, reads, inst, None))
            return inst
        return wrapped


class Prog:

    ENGINES = ("sync", "vector", "scalar", "tensor", "gpsimd")

    def __init__(self):
        self.nc = bass.Bass("TRN2", target_bir_lowering=False)
        self.es = ExitStack()
        self.sems = {}
        self.ops = {e: [] for e in self.ENGINES}
        self.pools = {}

    def sb(self, name, shape, dt):
        return self.es.enter_context(self.nc.sbuf_tensor(name, shape, dt))

    def ps(self, name, shape, dt=F32):
        return self.es.enter_context(self.nc.psum_tensor(name, shape, dt))

    def sem(self, name):
        h = self.es.enter_context(self.nc.semaphore(name))
        self.sems[name] = [h, 0]
        return h

    def inc(self, name, n):
        s = self.sems[name]
        s[1] += n
        return (s[0], s[1])

    def h(self, name):
        return self.sems[name][0]

    def val(self, name):
        return self.sems[name][1]

    def din(self, name, shape, dt):
        return self.nc.dram_tensor(name, shape, dt, kind="ExternalInput").ap()

    def dout(self, name, shape, dt):
        return self.nc.dram_tensor(name, shape, dt, kind="ExternalOutput").ap()

    def dint(self, name, shape, dt):
        return self.nc.dram_tensor(name, shape, dt).ap()

    def on(self, eng, fn):
        self.ops[eng].append(fn)

    def step(self, eng, semname, fn):
        h, v = self.inc(semname, 1)

        def wrapped(q, fn=fn, h=h, v=v):
            last = fn(q)
            assert last is not None, "step fn must return its last instruction"
            last.then_inc(h, 1)
            if isinstance(q, GuardQ):
                q.note_inc(last, h, v)
        self.ops[eng].append(wrapped)
        return v

    def dma_pool(self, base, n):
        for i in range(n):
            self.sem(f"{base}{i}")
        self.pools[base] = {"n": n, "next": 0}

    def dma(self, eng, pool, fn):
        st = self.pools[pool]
        i = st["next"]
        st["next"] = (i + 1) % st["n"]
        semname = f"{pool}{i}"
        s = self.sems[semname]
        h = s[0]
        prev = s[1]
        s[1] += 16
        tgt = s[1]

        def wrapped(q, fn=fn, h=h, prev=prev):
            if prev > 0:
                q.wait_ge(h, prev)
            inst = fn(q)
            assert inst is not None
            inst.then_inc(h, 16)
        self.ops[eng].append(wrapped)
        return (semname, tgt)

    def pool_totals(self, pool):
        st = self.pools[pool]
        return {f"{pool}{i}": self.sems[f"{pool}{i}"][1] for i in range(st["n"])}

    def emit_pool_wait(self, q, totals):
        for semname, v in totals.items():
            if v > 0:
                q.wait_ge(self.sems[semname][0], v)

    def build(self):
        nc = self.nc
        chains = {}
        for eng in ("vector", "scalar", "gpsimd", "tensor"):
            h = self.es.enter_context(nc.semaphore(f"_chain_{eng}"))
            chains[eng] = [h, 0]
        with nc.Block() as blk:
            if self.ops["sync"]:
                @blk.sync
                def _(q):
                    for fn in self.ops["sync"]:
                        fn(q)
            if self.ops["vector"]:
                @blk.vector
                def _(q):
                    gq = GuardQ(q, "vector", chains["vector"])
                    for fn in self.ops["vector"]:
                        fn(gq)
            if self.ops["scalar"]:
                @blk.scalar
                def _(q):
                    gq = GuardQ(q, "scalar", chains["scalar"])
                    for fn in self.ops["scalar"]:
                        fn(gq)
            if self.ops["tensor"]:
                @blk.tensor
                def _(q):
                    gq = GuardQ(q, "tensor", chains["tensor"])
                    for fn in self.ops["tensor"]:
                        fn(gq)
            if self.ops["gpsimd"]:
                @blk.gpsimd
                def _(q):
                    gq = GuardQ(q, "gpsimd", chains["gpsimd"])
                    for fn in self.ops["gpsimd"]:
                        fn(gq)
        return nc

    def close(self):
        self.es.close()


def host_consts():
    lt = np.triu(np.ones((128, 128), np.float32), 1)
    ident = np.eye(128, dtype=np.float32)
    iota32 = np.broadcast_to(np.arange(E0, dtype=np.float32), (128, E0)).copy()
    return lt, ident, iota32


class RhsStream:

    def __init__(self, p, name, shape, dt, n=3, pool="lx", gate_sem="ts"):
        self.p = p
        self.bufs = [p.sb(f"{name}{i}", shape, dt) for i in range(n)]
        self.n = n
        self.i = 0
        self.free_at = [0] * n
        self.pool = pool
        self.gate = gate_sem

    def load(self, src_ap, extra_wait=None):
        b = self.i % self.n
        self.i += 1
        w = self.free_at[b]
        g = self.gate

        try:
            r, c = src_ap.shape[0], src_ap.shape[1]
        except Exception:
            r, c = self.bufs[0].shape

        def f(q, src_ap=src_ap, b=b, w=w, extra_wait=extra_wait, r=r, c=c):
            if w:
                q.wait_ge(self.p.h(g), w)
            if extra_wait is not None:
                for s_ in extra_wait:
                    q.wait_ge(self.p.h(s_[0]), s_[1])
            return q.dma_start(out=self.bufs[b][0:r, 0:c], in_=src_ap)
        sem = self.p.dma("sync", self.pool, f)
        return (b, sem, r, c)

    def ap(self, handle):
        return self.bufs[handle[0]][0:handle[2], 0:handle[3]]

    def wait(self, q, handle):
        q.wait_ge(self.p.h(handle[1][0]), handle[1][1])

    def sem(self, handle):
        return handle[1]

    def set_free(self, handles, milestone):
        for h in handles:
            self.free_at[h[0]] = milestone


class OutRing:

    def __init__(self, p, name, shape, dt, n=2, pool="so", prod_sem="vs"):
        self.p = p
        self.bufs = [p.sb(f"{name}{i}", shape, dt) for i in range(n)]
        self.n = n
        self.i = 0
        self.last_store = [None] * n
        self.pool = pool
        self.prod_sem = prod_sem

    def acquire(self):
        b = self.i % self.n
        self.i += 1
        return b, self.last_store[b]

    def store(self, b, dst_ap, milestone, rows=None, cols=None):
        def f(q, b=b, dst_ap=dst_ap, milestone=milestone, rows=rows, cols=cols):
            q.wait_ge(self.p.h(self.prod_sem), milestone)
            src = self.bufs[b][:]
            if rows is not None or cols is not None:
                r = rows if rows is not None else self.bufs[b].shape[0]
                c = cols if cols is not None else self.bufs[b].shape[1]
                src = self.bufs[b][0:r, 0:c]
            return q.dma_start(out=dst_ap, in_=src)
        h = self.p.dma("sync", self.pool, f)
        self.last_store[b] = h
        return h





def build_A(core_id):
    p = Prog()
    x_all = p.din("x_all", [NTOK, D], F32)
    gwT = p.din("gwT", [D, E0], F32)
    lt_in = p.din("lt_in", [128, 128], F32)
    id_in = p.din("id_in", [128, 128], F32)
    io_in = p.din("io_in", [128, E0], F32)

    xn_tm = p.dout("xn_tm", [NTOK, D], BF16)
    stats_o = p.dout("stats", [2, NTOK], F32)
    member_o = p.dout("member", [NTOK, E0], BF16)
    Cmat_o = p.dout("Cmat", [NTOK, E0], F32)
    bases_o = p.dout("bases", [NT, E0], F32)
    slot_o = p.dout("slot", [NTOK, TOPK], F32)
    slotu_o = p.dout("slotu", [NTOK, TOPK], U32)
    wslot_o = p.dout("wslot", [NTOK, TOPK], F32)
    xt_own = p.dout("xt_own", [D, T], F32)

    DB = 2
    xt = [p.sb(f"xt{i}", [128, D], F32) for i in range(DB)]
    xTc = [p.sb(f"xTc{i}", [128, 8 * 128], F32) for i in range(DB)]
    xn16 = [p.sb(f"xn16{i}", [128, D], BF16) for i in range(DB)]
    sq = p.sb("sq", [128, D], F32)
    ltb = p.sb("ltb", [128, 128], BF16)
    ltf = p.sb("ltf", [128, 128], F32)
    ident = p.sb("ident", [128, 128], F32)
    iota32 = p.sb("iota32", [128, E0], F32)
    gwTs = p.sb("gwTs", [128, 8 * E0], F32)
    sneg = p.sb("sneg", [1, E0], F32)
    ones1r = p.sb("ones1r", [1, 128], F32)
    onescolf = p.sb("onescolf", [128, 1], F32)
    onescolb = p.sb("onescolb", [128, 1], BF16)
    ones32r = p.sb("ones32r", [1, E0], F32)
    base_row = p.sb("base_row", [1, E0], F32)
    consts = p.sb("consts", [128, 8], F32)
    mu = [p.sb(f"mu{i}", [128, 1], F32) for i in range(DB)]
    rstd = [p.sb(f"rstd{i}", [128, 1], F32) for i in range(DB)]
    ssum = p.sb("ssum", [128, 1], F32)
    msq = p.sb("msq", [128, 1], F32)
    varb = p.sb("varb", [128, 1], F32)
    lnv = p.sb("lnv", [128, 1], F32)
    statcol = [p.sb(f"statcol{i}", [128, 2], F32) for i in range(DB)]
    statrow = [p.sb(f"statrow{i}", [2, 128], F32) for i in range(DB)]
    lgev = [p.sb(f"lgev{i}", [E0, 128], F32) for i in range(DB)]
    lgT2 = [p.sb(f"lgT2{i}", [E0, 128], F32) for i in range(DB)]
    logits = [p.sb(f"logits{i}", [128, E0], F32) for i in range(DB)]
    t8 = [p.sb(f"t8{i}", [128, TOPK], F32) for i in range(DB)]
    m0neg = p.sb("m0neg", [128, 1], F32)
    idxu = p.sb("idxu", [128, TOPK], U32)
    idxf = [p.sb(f"idxf{i}", [128, TOPK], F32) for i in range(DB)]
    e8 = p.sb("e8", [128, TOPK], F32)
    den = p.sb("den", [128, 1], F32)
    rden = p.sb("rden", [128, 1], F32)
    wnorm = [p.sb(f"wnorm{i}", [128, TOPK], F32) for i in range(DB)]
    memf = p.sb("memf", [128, E0], F32)
    mk = p.sb("mk", [128, E0], F32)
    mem = [p.sb(f"mem{i}", [128, E0], BF16) for i in range(DB)]
    Ct_tmp = p.sb("Ct_tmp", [128, E0], F32)
    Ct = [p.sb(f"Ct{i}", [128, E0], F32) for i in range(DB)]
    posk = p.sb("posk", [128, TOPK], F32)
    keepk = p.sb("keepk", [128, TOPK], F32)
    slotm = [p.sb(f"slotm{i}", [128, TOPK], F32) for i in range(DB)]
    slotu = [p.sb(f"slotu{i}", [128, TOPK], U32) for i in range(DB)]
    wm = [p.sb(f"wm{i}", [128, TOPK], F32) for i in range(DB)]
    mtmp = [p.sb(f"mtmp{i}", [128, E0], F32) for i in range(3)]
    mk3 = [p.sb(f"mk3{i}", [128, E0], F32) for i in range(3)]
    slotn = p.sb("slotn", [128, TOPK], F32)
    sdummy = p.sb("sdummy", [128, 1], F32)
    sdummy2 = p.sb("sdummy2", [128, 1], F32)

    ptr = [p.ps(f"ptr{i}", [128, 128], F32) for i in range(2)]
    pbank = [p.ps(f"pbank{i}", [128, 512], F32) for i in range(DB)]
    plog = [pbank[i][0:E0, 0:128] for i in range(DB)]
    pstat = [pbank[i][0:2, 128:256] for i in range(DB)]
    plg2 = [pbank[i][:, 256:256 + E0] for i in range(DB)]
    pct = [pbank[i][:, 288:288 + E0] for i in range(DB)]
    pbase = [pbank[i][:, 320:320 + E0] for i in range(DB)]
    pcs2 = pbank[0][0:1, 352:352 + E0]
    pcs = p.ps("pcs", [1, E0], F32)

    for nm in ["vs", "ss", "ts", "bs"]:
        p.sem(nm)
    p.dma_pool("ld", 4)
    p.dma_pool("so", 8)


    for ap_, t_ in [(lt_in, ltf), (id_in, ident), (io_in, iota32)]:
        p.dma("sync", "ld", lambda q, a=ap_, t2=t_: q.dma_start(out=t2[:], in_=a[:, :]))
    for c in range(8):
        p.dma("sync", "ld", lambda q, c=c: q.dma_start(
            out=gwTs[:, c * E0:(c + 1) * E0], in_=gwT[c * 128:(c + 1) * 128, :]))
    cst_done = p.pool_totals("ld")

    def vinit(q):
        q.memset(ones1r[:], 1.0)
        q.memset(onescolf[:], 1.0)
        q.memset(onescolb[:], 1.0)
        q.memset(ones32r[:], 1.0)
        q.memset(base_row[:], 0.0)
        q.memset(consts[:, 0:1], 1e-5)
        q.memset(consts[:, 1:2], -0.5)
        p.emit_pool_wait(q, cst_done)
        return q.tensor_copy(out=ltb[:], in_=ltf[:])
    vs_init = p.step("vector", "vs", vinit)

    def t_sneg(q, w=vs_init):
        q.wait_ge(p.h("vs"), w)
        for c in range(8):
            last = q.matmul(pcs[:], onescolf[:], gwTs[:, c * E0:(c + 1) * E0], start=(c == 0), stop=(c == 7))
        return last
    ts_sneg = p.step("tensor", "ts", t_sneg)

    def v_sneg(q, w=ts_sneg):
        q.wait_ge(p.h("ts"), w)
        return q.tensor_scalar(out=sneg[:], in0=pcs[:], scalar1=-1.0, scalar2=None, op0=ALU.mult)
    vs_sneg = p.step("vector", "vs", v_sneg)

    own_lo = core_id * TILES_PER_CORE
    own_hi = own_lo + TILES_PER_CORE
    ptr_evac = [0, 0]
    tile_free = {}
    so_end = {}
    bs_prev = 0

    import os
    NTRUN = int(os.environ.get('NTRUN', NT))
    for t in range(NTRUN):
        b = t % DB
        wv = tile_free.get(t - DB, 0)

        def ld_x(q, b=b, t=t, wv=wv):
            if wv:
                q.wait_ge(p.h("vs"), wv)
            return q.dma_start(out=xt[b][:], in_=x_all[t * 128:(t + 1) * 128, :])
        ldsem, ldv = p.dma("sync", "ld", ld_x)

        wso = so_end.get(t - DB, None)

        def v_stats(q, b=b, ldsem=ldsem, ldv=ldv, wso=wso):
            if wso is not None:
                p.emit_pool_wait(q, wso)
            q.wait_ge(p.h(ldsem), ldv)
            q.tensor_mul(out=sq[:], in0=xt[b][:], in1=xt[b][:])
            q.reduce_sum(out=ssum[:], in_=xt[b][:], axis=AX)
            q.reduce_sum(out=msq[:], in_=sq[:], axis=AX)
            q.tensor_scalar(out=mu[b][:], in0=ssum[:], scalar1=1.0 / D, scalar2=None, op0=ALU.mult)
            q.tensor_scalar(out=msq[:], in0=msq[:], scalar1=1.0 / D, scalar2=None, op0=ALU.mult)
            q.tensor_mul(out=varb[:], in0=mu[b][:], in1=mu[b][:])
            return q.tensor_sub(out=varb[:], in0=msq[:], in1=varb[:])
        vs_stats = p.step("vector", "vs", v_stats)

        def s_rstd(q, b=b, w=vs_stats):
            q.wait_ge(p.h("vs"), w)
            q.activation(out=lnv[:], in_=varb[:], func=AF.Ln, bias=consts[:, 0:1], scale=1.0)
            return q.activation(out=rstd[b][:], in_=lnv[:], func=AF.Exp, bias=0.0, scale=consts[:, 1:2])
        ss_rstd = p.step("scalar", "ss", s_rstd)

        def v_xn(q, b=b, w=ss_rstd):
            q.wait_ge(p.h("ss"), w)
            q.tensor_copy(out=statcol[b][:, 0:1], in_=mu[b][:])
            q.tensor_copy(out=statcol[b][:, 1:2], in_=rstd[b][:])
            return q.tensor_scalar(out=xn16[b][:], in0=xt[b][:], scalar1=mu[b][:], scalar2=rstd[b][:],
                                   op0=ALU.subtract, op1=ALU.mult)
        vs_xn = p.step("vector", "vs", v_xn)

        def st_xn(q, b=b, t=t, w=vs_xn):
            q.wait_ge(p.h("vs"), w)
            return q.dma_start(out=xn_tm[t * 128:(t + 1) * 128, :], in_=xn16[b][:])
        p.dma("sync", "so", st_xn)

        vs_ev_last = 0
        for c in range(8):
            pb = c % 2
            wv_evac = ptr_evac[pb]

            def t_tr(q, b=b, c=c, pb=pb, ldsem=ldsem, ldv=ldv, wv=wv_evac):
                if c == 0:
                    q.wait_ge(p.h(ldsem), ldv)
                if wv:
                    q.wait_ge(p.h("vs"), wv)
                return q.transpose(ptr[pb][:], xt[b][:, c * 128:(c + 1) * 128], ident[:])
            ts_tr = p.step("tensor", "ts", t_tr)

            def v_ev(q, b=b, c=c, pb=pb, w=ts_tr):
                q.wait_ge(p.h("ts"), w)
                return q.tensor_copy(out=xTc[b][:, c * 128:(c + 1) * 128], in_=ptr[pb][:])
            vs_ev = p.step("vector", "vs", v_ev)
            ptr_evac[pb] = vs_ev
            vs_ev_last = vs_ev
        vs_xT_done = vs_ev_last

        def t_stat(q, b=b, w=vs_xn):
            q.wait_ge(p.h("vs"), w)
            return q.transpose(pstat[b], statcol[b][:], ident[:])
        ts_stat = p.step("tensor", "ts", t_stat)

        def v_statrow(q, b=b, w=ts_stat):
            q.wait_ge(p.h("ts"), w)
            return q.tensor_copy(out=statrow[b][:], in_=pstat[b])
        vs_statrow = p.step("vector", "vs", v_statrow)

        def st_stat(q, b=b, t=t, w=vs_statrow):
            q.wait_ge(p.h("vs"), w)
            return q.dma_start(out=stats_o[:, t * 128:(t + 1) * 128], in_=statrow[b][:])
        p.dma("sync", "so", st_stat)

        def t_router(q, b=b, w1=vs_xT_done, w2=vs_statrow):
            q.wait_ge(p.h("vs"), w1)
            q.wait_ge(p.h("vs"), w2)
            for c in range(8):
                q.matmul(plog[b], gwTs[:, c * E0:(c + 1) * E0], xTc[b][:, c * 128:(c + 1) * 128],
                         start=(c == 0), stop=False)
            return q.matmul(plog[b], sneg[:], statrow[b][0:1, :], start=False, stop=True)
        ts_router = p.step("tensor", "ts", t_router)

        def v_lg(q, b=b, w=ts_router):
            q.wait_ge(p.h("ts"), w)
            return q.tensor_copy(out=lgT2[b][:], in_=plog[b])
        vs_lg = p.step("vector", "vs", v_lg)

        def t_lgT(q, b=b, w=vs_lg):
            q.wait_ge(p.h("vs"), w)
            return q.transpose(plg2[b], lgT2[b][:], ident[0:E0, 0:E0])
        ts_lgT = p.step("tensor", "ts", t_lgT)

        def v_top(q, b=b, w=ts_lgT):
            q.wait_ge(p.h("ts"), w)
            q.tensor_scalar(out=logits[b][:], in0=plg2[b], scalar1=rstd[b][:], scalar2=None, op0=ALU.mult)
            q.max(out=t8[b][:], in_=logits[b][:])
            q.max_index(out=idxu[:], in_max=t8[b][:], in_values=logits[b][:])
            q.tensor_scalar(out=m0neg[:], in0=t8[b][:, 0:1], scalar1=-1.0, scalar2=None, op0=ALU.mult)
            return q.tensor_copy(out=idxf[b][:], in_=idxu[:])
        vs_top = p.step("vector", "vs", v_top)

        def s_exp(q, b=b, w=vs_top):
            q.wait_ge(p.h("vs"), w)
            return q.activation(out=e8[:], in_=t8[b][:], func=AF.Exp, bias=m0neg[:], scale=1.0, accum_out=den[:])
        ss_exp = p.step("scalar", "ss", s_exp)

        def v_w(q, b=b, w=ss_exp):
            q.wait_ge(p.h("ss"), w)
            q.reciprocal(out=rden[:], in_=den[:])
            q.memset(memf[:], 0.0)
            q.tensor_scalar(out=wnorm[b][:], in0=e8[:], scalar1=rden[:], scalar2=None, op0=ALU.mult)
            q.tensor_scalar(out=mk3[0][:], in0=iota32[:], scalar1=idxf[b][:, 0:1], scalar2=None,
                            op0=ALU.is_equal)
            for k in range(1, TOPK):
                q.tensor_scalar(out=mk3[k % 2][:], in0=iota32[:], scalar1=idxf[b][:, k:k + 1],
                                scalar2=None, op0=ALU.is_equal)
                q.tensor_add(out=memf[:], in0=memf[:], in1=mk3[(k - 1) % 2][:])
            q.tensor_add(out=memf[:], in0=memf[:], in1=mk3[(TOPK - 1) % 2][:])
            return q.tensor_copy(out=mem[b][:], in_=memf[:])
        vs_mem = p.step("vector", "vs", v_w)

        def st_base(q, t=t, w=bs_prev):
            if w:
                q.wait_ge(p.h("bs"), w)
            return q.dma_start(out=bases_o[t:t + 1, :], in_=base_row[:])
        so_base = p.dma("sync", "so", st_base)

        def t_C(q, b=b, w=vs_mem):
            q.wait_ge(p.h("vs"), w)
            q.matmul(pct[b], ltb[:], mem[b][:], start=True, stop=True)
            q.matmul(pbase[b], ones1r[:], base_row[:], start=True, stop=True)
            return q.matmul(pcs2, onescolb[:], mem[b][:], start=True, stop=True)
        ts_C = p.step("tensor", "ts", t_C)

        def v_C(q, b=b, w=ts_C, wso=so_base):
            q.wait_ge(p.h("ts"), w)
            q.wait_ge(p.h(wso[0]), wso[1])
            q.tensor_copy(out=Ct_tmp[:], in_=pct[b])
            q.tensor_add(out=Ct[b][:], in0=Ct_tmp[:], in1=pbase[b])
            return q.tensor_add(out=base_row[:], in0=base_row[:], in1=pcs2)
        vs_C = p.step("vector", "vs", v_C)

        def v_bs(q):
            return q.memset(sdummy2[:], 0.0)
        bs_prev = p.step("vector", "bs", v_bs)

        def v_pos(q, b=b):
            for s in range(TOPK + 2):
                if s < TOPK:
                    q.tensor_scalar(out=mk3[s % 3][:], in0=iota32[:], scalar1=idxf[b][:, s:s + 1],
                                    scalar2=None, op0=ALU.is_equal)
                if 1 <= s < TOPK + 1:
                    k = s - 1
                    q.tensor_mul(out=mtmp[k % 3][:], in0=mk3[k % 3][:], in1=Ct[b][:])
                if s >= 2:
                    k = s - 2
                    q.reduce_sum(out=posk[:, k:k + 1], in_=mtmp[k % 3][:], axis=AX)
            q.tensor_scalar(out=slotn[:], in0=idxf[b][:], scalar1=float(CAP), scalar2=None, op0=ALU.mult)
            q.tensor_scalar(out=keepk[:], in0=posk[:], scalar1=float(CAP), scalar2=None, op0=ALU.is_lt)
            q.tensor_add(out=slotn[:], in0=slotn[:], in1=posk[:])
            q.tensor_mul(out=wm[b][:], in0=wnorm[b][:], in1=keepk[:])
            q.tensor_scalar(out=slotn[:], in0=slotn[:], scalar1=60000.0, scalar2=None, op0=ALU.subtract)
            q.tensor_mul(out=slotn[:], in0=slotn[:], in1=keepk[:])
            q.tensor_scalar(out=slotm[b][:], in0=slotn[:], scalar1=60000.0, scalar2=None, op0=ALU.add)
            return q.tensor_copy(out=slotu[b][:], in_=slotm[b][:])
        vs_pos = p.step("vector", "vs", v_pos)

        for (dst, src, w) in [
            (member_o[t * 128:(t + 1) * 128, :], mem[b], vs_mem),
            (Cmat_o[t * 128:(t + 1) * 128, :], Ct[b], vs_C),
            (slot_o[t * 128:(t + 1) * 128, :], slotm[b], vs_pos),
            (slotu_o[t * 128:(t + 1) * 128, :], slotu[b], vs_pos),
            (wslot_o[t * 128:(t + 1) * 128, :], wm[b], vs_pos),
        ]:
            def st_g(q, dst=dst, src=src, w=w):
                q.wait_ge(p.h("vs"), w)
                return q.dma_start(out=dst, in_=src[:])
            p.dma("sync", "so", st_g)

        if own_lo <= t < own_hi:
            tl = t - own_lo
            for c in range(8):
                def st_xT(q, b=b, c=c, tl=tl, w=vs_xT_done):
                    q.wait_ge(p.h("vs"), w)
                    return q.dma_start(out=xt_own[c * 128:(c + 1) * 128, tl * 128:(tl + 1) * 128],
                                       in_=xTc[b][:, c * 128:(c + 1) * 128])
                p.dma("sync", "so", st_xT)

        tile_free[t] = vs_pos
        so_end[t] = p.pool_totals("so")

    so_total = p.pool_totals("so")
    ld_total = p.pool_totals("ld")
    p.on("gpsimd", lambda q: p.emit_pool_wait(q, so_total))
    p.on("gpsimd", lambda q: p.emit_pool_wait(q, ld_total))

    nc = p.build()
    return p, nc






def build_B():
    p = Prog()
    xn_tm = p.din("xn_tm", [NTOK, D], BF16)
    slotu_i = p.din("slotu", [NTOK, TOPK], U32)
    buf_o = p.dout("buf", [NSLOT, D], BF16)

    xstg = p.sb("xstg", [128, D], BF16)
    islot = p.sb("islot", [128, TOPK], U32)
    zt = p.sb("zt", [128, D], BF16)

    p.sem("vs")
    stg = p.sem("stg")
    wr = p.sem("wr")
    p.dma_pool("so", 4)

    def vinit(q):
        return q.memset(zt[:], 0.0)
    vs_init = p.step("vector", "vs", vinit)

    for i in range(NSLOT // 128):
        def st_z(q, i=i, w=vs_init):
            if i == 0:
                q.wait_ge(p.h("vs"), w)
            return q.dma_start(out=buf_o[i * 128:(i + 1) * 128, :], in_=zt[:])
        p.dma("gpsimd", "so", st_z)
    zero_done = p.pool_totals("so")

    def emit_loads(q):
        for t in range(NT):
            if t >= 1:
                q.wait_ge(p.h("wr"), 16 * TOPK * t)
            q.dma_start(out=xstg[:], in_=xn_tm[t * 128:(t + 1) * 128, :]).then_inc(p.h("stg"), 16)
            q.dma_start(out=islot[:], in_=slotu_i[t * 128:(t + 1) * 128, :]).then_inc(p.h("stg"), 16)
    p.on("sync", emit_loads)

    def g_loop(q):
        p.emit_pool_wait(q, zero_done)
        with q.Fori(0, NT) as i:
            q.wait_ge(p.h("stg"), (i + 1) * 32)
            for k in range(TOPK):
                q.indirect_dma_start(
                    out=buf_o[:, :],
                    out_offset=IndirectOffsetOnAxis(ap=islot[:, k:k + 1], axis=0),
                    in_=xstg[:, :], in_offset=None,
                    bounds_check=NSLOT - 1, oob_is_err=False,
                ).then_inc(p.h("wr"), 16)
            q.wait_ge(p.h("wr"), (i + 1) * 16 * TOPK)
        q.wait_ge(p.h("wr"), 16 * TOPK * NT)
    p.on("gpsimd", g_loop)

    nc = p.build()
    return p, nc





FF = 1536
F_ = 768
NHC = F_ // 128


def build_D():
    p = Prog()
    buf_i = p.din("buf", [NSLOT, D], BF16)
    gupT_i = p.din("gupT", [E0, D, FF], F32)
    dnT_i = p.din("dnT", [E0, F_, D], F32)
    eo_o = p.dout("eo", [NSLOT, D], BF16)

    DBW = 2
    gupb = [p.sb(f"gupb{i}", [128, 8 * FF], BF16) for i in range(DBW)]
    dnb = [p.sb(f"dnb{i}", [128, NHC * D], BF16) for i in range(DBW)]
    bufT = [p.sb(f"bufT{i}", [128, 8 * CAP], BF16) for i in range(DBW)]
    hT = p.sb("hT", [128, NHC * CAP], BF16)
    sg = p.sb("sg", [128, CAP], F32)
    eost = [p.sb(f"eost{i}", [128, D], BF16) for i in range(2)]

    pg = [p.ps(f"pg{i}", [128, 512], F32) for i in range(2)]
    po = [p.ps(f"po{i}", [128, 512], F32) for i in range(4)]

    for nm in ["vs", "ss", "ts"]:
        p.sem(nm)
    p.dma_pool("lw", 4)
    p.dma_pool("lb", 4)
    p.dma_pool("so", 4)

    def load_weights(e, w, wait_ts):
        outs = []
        for c in range(8):
            def f(q, e=e, w=w, c=c, wait_ts=wait_ts):
                if wait_ts and c == 0:
                    q.wait_ge(p.h("ts"), wait_ts)
                return q.dma_start(out=gupb[w][:, c * FF:(c + 1) * FF],
                                   in_=gupT_i[e, c * 128:(c + 1) * 128, :])
            outs.append(p.dma("gpsimd", "lw", f))
        for fc in range(NHC):
            def f(q, e=e, w=w, fc=fc, wait_ts=wait_ts):
                if wait_ts and fc == 0:
                    q.wait_ge(p.h("ts"), wait_ts)
                return q.dma_start(out=dnb[w][:, fc * D:(fc + 1) * D],
                                   in_=dnT_i[e, fc * 128:(fc + 1) * 128, :])
            outs.append(p.dma("gpsimd", "lw", f))
        return outs

    def load_bufT(e, w, wait_ts):
        outs = []
        for c in range(8):
            for sh in range(2):
                def f(q, e=e, w=w, c=c, sh=sh, wait_ts=wait_ts):
                    if wait_ts and c == 0 and sh == 0:
                        q.wait_ge(p.h("ts"), wait_ts)
                    return q.dma_start(
                        out=bufT[w][:, c * CAP + sh * 512: c * CAP + (sh + 1) * 512],
                        in_=buf_i[e * CAP + sh * 512:e * CAP + (sh + 1) * 512,
                                  c * 128:(c + 1) * 128],
                        transpose=True)
                outs.append(p.dma("sync", "lb", f))
        return outs

    wload = {0: load_weights(0, 0, 0)}
    bload = {0: load_bufT(0, 0, 0)}
    ts_end = {-1: 0}
    pg_sil_prev = 0
    pg_h_prev = 0
    po_free = [None, None]
    eo_free = [None, None]
    hT_stage2_prev = 0

    for e in range(E0):
        w = e % DBW
        lw = wload[e]
        lb = bload[e]

        vs_h_last = 0
        for fc in range(NHC):
            for sh in range(2):
                def t_gu(q, w=w, fc=fc, sh=sh, lw=lw, lb=lb, first=(fc == 0 and sh == 0),
                         wss=pg_sil_prev, wvs=pg_h_prev):
                    if first:
                        for s_ in lw:
                            q.wait_ge(p.h(s_[0]), s_[1])
                        for s_ in lb:
                            q.wait_ge(p.h(s_[0]), s_[1])
                    if wss:
                        q.wait_ge(p.h("ss"), wss)
                    if wvs:
                        q.wait_ge(p.h("vs"), wvs)
                    for c in range(8):
                        q.matmul(pg[0][:],
                                 gupb[w][:, c * FF + fc * 128:c * FF + (fc + 1) * 128],
                                 bufT[w][:, c * CAP + sh * 512:c * CAP + (sh + 1) * 512],
                                 start=(c == 0), stop=(c == 7))
                    for c in range(8):
                        last = q.matmul(pg[1][:],
                                        gupb[w][:, c * FF + (fc + NHC) * 128:c * FF + (fc + NHC + 1) * 128],
                                        bufT[w][:, c * CAP + sh * 512:c * CAP + (sh + 1) * 512],
                                        start=(c == 0), stop=(c == 7))
                    return last
                ts_gu = p.step("tensor", "ts", t_gu)

                def s_silu(q, sh=sh, w1=ts_gu):
                    q.wait_ge(p.h("ts"), w1)
                    return q.activation(out=sg[:, sh * 512:(sh + 1) * 512], in_=pg[0][:],
                                        func=AF.Silu, bias=0.0, scale=1.0)
                ss_silu = p.step("scalar", "ss", s_silu)
                pg_sil_prev = ss_silu

                def v_h(q, fc=fc, sh=sh, w1=ts_gu, w2=ss_silu,
                        wh=(hT_stage2_prev if (fc == 0 and sh == 0) else 0)):
                    q.wait_ge(p.h("ts"), w1)
                    q.wait_ge(p.h("ss"), w2)
                    if wh:
                        q.wait_ge(p.h("ts"), wh)
                    return q.tensor_tensor(
                        out=hT[:, fc * CAP + sh * 512:fc * CAP + (sh + 1) * 512],
                        in0=sg[:, sh * 512:(sh + 1) * 512], in1=pg[1][:], op=ALU.mult)
                vs_h = p.step("vector", "vs", v_h)
                pg_h_prev = vs_h
                vs_h_last = vs_h

        if e + 1 < E0:
            wload[e + 1] = load_weights(e + 1, (e + 1) % DBW, ts_end[e - 1])
            bload[e + 1] = load_bufT(e + 1, (e + 1) % DBW, ts_end[e - 1])

        for st in range(8):
            j = st % 2

            def t_o(q, w=w, st=st, j=j, whl=vs_h_last, wpo=po_free[j]):
                q.wait_ge(p.h("vs"), whl)
                if wpo is not None:
                    q.wait_ge(p.h("vs"), wpo[0])
                    q.wait_ge(p.h("ss"), wpo[1])
                for fc in range(NHC):
                    q.matmul(po[2 * j][:], hT[:, fc * CAP + st * 128:fc * CAP + (st + 1) * 128],
                             dnb[w][:, fc * D:fc * D + 512],
                             start=(fc == 0), stop=(fc == NHC - 1))
                for fc in range(NHC):
                    last = q.matmul(po[2 * j + 1][:], hT[:, fc * CAP + st * 128:fc * CAP + (st + 1) * 128],
                                    dnb[w][:, fc * D + 512:fc * D + 1024],
                                    start=(fc == 0), stop=(fc == NHC - 1))
                    if fc == NHC - 1:
                        return last
            ts_o = p.step("tensor", "ts", t_o)
            hT_stage2_prev = ts_o

            def v_eo(q, j=j, w1=ts_o, wf=eo_free[j]):
                q.wait_ge(p.h("ts"), w1)
                if wf is not None:
                    q.wait_ge(p.h(wf[0]), wf[1])
                return q.tensor_copy(out=eost[j][:, 0:512], in_=po[2 * j][:])
            vs_eo = p.step("vector", "vs", v_eo)

            def s_eo(q, j=j, w1=ts_o, wf=eo_free[j]):
                q.wait_ge(p.h("ts"), w1)
                if wf is not None:
                    q.wait_ge(p.h(wf[0]), wf[1])
                return q.copy(out=eost[j][:, 512:1024], in_=po[2 * j + 1][:])
            ss_eo = p.step("scalar", "ss", s_eo)
            po_free[j] = (vs_eo, ss_eo)

            def st_eo(q, e=e, st=st, j=j, w1=vs_eo, w2=ss_eo):
                q.wait_ge(p.h("vs"), w1)
                q.wait_ge(p.h("ss"), w2)
                return q.dma_start(out=eo_o[e * CAP + st * 128:e * CAP + (st + 1) * 128, :],
                                   in_=eost[j][:])
            eo_free[j] = p.dma("sync", "so", st_eo)

        ts_end[e] = p.val("ts")

    so_t = p.pool_totals("so")
    lw_t = p.pool_totals("lw")
    lb_t = p.pool_totals("lb")
    def fin(q):
        p.emit_pool_wait(q, so_t)
        p.emit_pool_wait(q, lw_t)
        p.emit_pool_wait(q, lb_t)
    p.on("gpsimd", fin)
    nc = p.build()
    return p, nc






def build_E1(core_id):
    p = Prog()
    eo_i = p.din("eo", [NSLOT, D], BF16)
    slotu_i = p.din("slotu", [T, TOPK], U32)
    wslot_i = p.din("wslot", [T, TOPK], F32)
    ffn0_o = p.dout("ffn0", [T, D], F32)

    islot = p.sb("islot", [128, TOPK], U32)
    wt = [p.sb(f"wt{i}", [128, TOPK], F32) for i in range(2)]
    gst = [p.sb(f"gst{k}", [128, D], F32) for k in range(TOPK)]
    acc = [p.sb(f"acc{i}", [128, D], F32) for i in range(2)]

    p.sem("vs")
    stg = p.sem("stg")
    gt = p.sem("gt")
    p.dma_pool("lw", 2)
    p.dma_pool("so", 4)

    own_lo = 0

    def vinit(q):
        for k in range(TOPK):
            q.memset(gst[k][:], 0.0)
        return q.memset(acc[0][:], 0.0)
    vs_init = p.step("vector", "vs", vinit)

    def g_loop(q):
        q.wait_ge(p.h("vs"), vs_init)
        with q.Fori(0, TILES_PER_CORE) as i:
            q.wait_ge(p.h("stg"), (i + 1) * 16)
            q.wait_ge(p.h("vs"), vs_init + i)
            for k in range(TOPK):
                q.indirect_dma_start(
                    out=gst[k][:, :], out_offset=None,
                    in_=eo_i[:, :], in_offset=IndirectOffsetOnAxis(ap=islot[:, k:k + 1], axis=0),
                    bounds_check=NSLOT - 1, oob_is_err=False,
                ).then_inc(p.h("gt"), 16)
            q.wait_ge(p.h("gt"), (i + 1) * 16 * TOPK)
        q.wait_ge(p.h("gt"), 16 * TOPK * TILES_PER_CORE)
    p.on("gpsimd", g_loop)

    so_free = {}
    for tl in range(TILES_PER_CORE):
        t = own_lo + tl
        b = tl % 2

        def ld_islot(q, t=t, tl=tl):
            if tl >= 1:
                q.wait_ge(p.h("gt"), 16 * TOPK * tl)
            return q.dma_start(out=islot[:], in_=slotu_i[t * 128:(t + 1) * 128, :])
        h, v = p.inc("stg", 16)

        def ld_islot_w(q, f=ld_islot, h=h, prev=v - 16):
            if prev > 0:
                q.wait_ge(h, prev)
            f(q).then_inc(h, 16)
        p.on("sync", ld_islot_w)

        def f(q, t=t, tl=tl, b=b):
            if tl >= 2:
                q.wait_ge(p.h("vs"), vs_init + tl - 1)
            return q.dma_start(out=wt[b][:], in_=wslot_i[t * 128:(t + 1) * 128, :])
        wlv = p.dma("sync", "lw", f)

        def v_acc(q, tl=tl, b=b, wlv=wlv, wso=so_free.get(tl - 2)):
            q.wait_ge(p.h(wlv[0]), wlv[1])
            if wso is not None:
                q.wait_ge(p.h(wso[0]), wso[1])
            q.wait_ge(p.h("gt"), 16 * TOPK * (tl + 1))
            for k in range(TOPK):
                if k == 0:
                    q.tensor_scalar(out=acc[b][:], in0=gst[0][:], scalar1=wt[b][:, 0:1],
                                    scalar2=None, op0=ALU.mult)
                else:
                    last = q.scalar_tensor_tensor(out=acc[b][:], in0=gst[k][:], scalar=wt[b][:, k:k + 1],
                                                  in1=acc[b][:], op0=ALU.mult, op1=ALU.add)
            return last
        vs_acc = p.step("vector", "vs", v_acc)

        def st_f(q, tl=tl, b=b, w=vs_acc):
            q.wait_ge(p.h("vs"), w)
            return q.dma_start(out=ffn0_o[tl * 128:(tl + 1) * 128, :], in_=acc[b][:])
        so_free[tl] = p.dma("sync", "so", st_f)

    so_t = p.pool_totals("so")
    p.on("gpsimd", lambda q: p.emit_pool_wait(q, so_t))
    nc = p.build()
    return p, nc





NH, HD, RD, QKD = 16, 64, 32, 96
KVR, QR = 256, 512
SCALE = float(1.0 / np.sqrt(96.0))
NC = 8


def build_E2a(core_id):
    p = Prog()
    xt_i = p.din("xt_own", [D, T], F32)
    stats_i = p.din("stats", [2, T], F32)
    Wkv_i = p.din("Wkv", [D, KVR], F32)
    gkv_i = p.din("gkv", [KVR, 1], F32)
    Wkdec_i = p.din("Wkdec", [KVR, D], F32)
    Wvdec_i = p.din("Wvdec", [KVR, D], F32)
    Wq_i = p.din("Wq", [D, QR], F32)
    Wqdec_i = p.din("Wqdec", [QR, D], F32)
    Wkrope_i = p.din("Wkrope", [D, NH * RD], F32)
    Wqrope_i = p.din("Wqrope", [QR, NH * RD], F32)
    cos_i = p.din("cosT", [RD, T], F32)
    sin_i = p.din("sinT", [RD, T], F32)

    KT_o = p.dout("KT", [NH * QKD, T], BF16)
    QT_o = p.dout("QT", [NH * QKD, T], BF16)
    V_o = p.dout("V", [T, NH * HD], BF16)
    xnT_s = p.dout("xnT_s", [D, T], BF16)
    qc_s = p.dout("qc_s", [QR, T], BF16)
    kvcn_s = p.dout("kvcn_s", [KVR, T], BF16)

    xgen = p.sb("xgen", [128, 512], F32)
    rsb = p.sb("rsb", [128, 512], F32)
    mub = p.sb("mub", [128, 512], F32)
    xsub = p.sb("xsub", [128, 512], F32)
    murow = p.sb("murow", [1, T], F32)
    rsrow = p.sb("rsrow", [1, T], F32)
    mursrow = p.sb("mursrow", [1, T], F32)
    wst = p.sb("wst", [128, 2048], BF16)
    wst2 = p.sb("wst2", [128, 4096], BF16)
    wst3 = p.sb("wst3", [128, 2048], BF16)
    gkv = [p.sb(f"gkvs{i}", [128, 1], F32) for i in range(2)]
    cosb = p.sb("cosb", [RD, T], BF16)
    sinb = p.sb("sinb", [RD, T], BF16)
    sqst = p.sb("sqst", [128, 512], F32)
    msrow = p.sb("msrow", [1, T], F32)
    rs2row = p.sb("rs2row", [1, T], F32)
    onesr = p.sb("onesr", [1, 128], F32)
    onescf = p.sb("onescf", [128, 1], F32)
    rs2b = p.sb("rs2b", [128, 512], F32)
    krp = p.sb("krp", [RD, T], BF16)
    krr = p.sb("krr", [RD, T], BF16)
    krt2 = p.sb("krt2", [RD, T], BF16)
    consts = p.sb("constsa", [128, 4], F32)
    wrp = p.sb("wrp", [128, 4096], BF16)
    wrp2 = p.sb("wrp2", [128, 4096], BF16)

    pa = p.ps("paa", [128, 512], F32)
    pb_ = p.ps("pba", [128, 512], F32)
    psm = p.ps("psma", [1, 512], F32)

    for nm in ["vs", "ss", "ts"]:
        p.sem(nm)
    p.dma_pool("ld", 2)
    p.dma_pool("lx", 4)
    p.dma_pool("lw", 2)
    p.dma_pool("so", 6)

    own0 = 0
    rs_x = RhsStream(p, "rsx", [128, 512], BF16, n=10, pool="lx", gate_sem="ts")
    oring = OutRing(p, "orng", [128, 512], BF16, n=3, pool="so", prod_sem="vs")

    cl = []
    cl.append(p.dma("sync", "ld", lambda q: q.dma_start(out=murow[:], in_=stats_i[0:1, own0:own0 + T])))
    cl.append(p.dma("sync", "ld", lambda q: q.dma_start(out=rsrow[:], in_=stats_i[1:2, own0:own0 + T])))
    cl.append(p.dma("gpsimd", "ld", lambda q: q.dma_start(out=cosb[:], in_=cos_i[:, :])))
    cl.append(p.dma("gpsimd", "ld", lambda q: q.dma_start(out=sinb[:], in_=sin_i[:, :])))
    for i in range(2):
        cl.append(p.dma("sync", "ld", lambda q, i=i: q.dma_start(
            out=gkv[i][:], in_=gkv_i[i * 128:(i + 1) * 128, :])))

    def vinit(q):
        q.memset(onesr[:], 1.0)
        q.memset(onescf[:], 1.0)
        q.memset(msrow[:], 0.0)
        q.memset(consts[:, 0:1], 1e-6)
        q.memset(consts[:, 1:2], -0.5)
        for s_ in cl:
            q.wait_ge(p.h(s_[0]), s_[1])
        return q.tensor_mul(out=mursrow[:], in0=murow[:], in1=rsrow[:])
    vs_init = p.step("vector", "vs", vinit)

    xg_sem_prev = 0
    for ncc in range(NC):
        def t_bc(q, ncc=ncc, w=vs_init):
            if ncc == 0:
                q.wait_ge(p.h("vs"), w)
            else:
                q.wait_ge(p.h("vs"), w)
            q.matmul(pa[:], onesr[:], rsrow[:, ncc * 512:(ncc + 1) * 512], start=True, stop=True)
            return q.matmul(pb_[:], onesr[:], murow[:, ncc * 512:(ncc + 1) * 512], start=True, stop=True)
        ts_bc = p.step("tensor", "ts", t_bc)

        def v_bc(q, w=ts_bc):
            q.wait_ge(p.h("ts"), w)
            q.tensor_copy(out=rsb[:], in_=pa[:])
            return q.tensor_copy(out=mub[:], in_=pb_[:])
        vs_bc = p.step("vector", "vs", v_bc)
        vs_init = vs_bc

        for cc in range(8):
            lg = p.dma("sync", "ld", lambda q, cc=cc, ncc=ncc, w=xg_sem_prev:
                       ((q.wait_ge(p.h("vs"), w) if w else None),
                        q.dma_start(out=xgen[:],
                                    in_=xt_i[cc * 128:(cc + 1) * 128, ncc * 512:(ncc + 1) * 512]))[-1])
            ob, oprev = oring.acquire()

            def v_xn(q, lg=lg, w=vs_bc, ob=ob, oprev=oprev):
                q.wait_ge(p.h(lg[0]), lg[1])
                q.wait_ge(p.h("vs"), w)
                if oprev is not None:
                    q.wait_ge(p.h(oprev[0]), oprev[1])
                q.tensor_sub(out=xsub[:], in0=xgen[:], in1=mub[:])
                return q.tensor_tensor(out=oring.bufs[ob][:], in0=xsub[:], in1=rsb[:], op=ALU.mult)
            vs_xn = p.step("vector", "vs", v_xn)
            xg_sem_prev = vs_xn
            oring.store(ob, xnT_s[cc * 128:(cc + 1) * 128, ncc * 512:(ncc + 1) * 512], vs_xn)
    xn_done = p.pool_totals("so")

    def load_wgrid(dst, src_ap, nrc, ncols, wait=None):
        sems = []
        ts_gate = p.val("ts")
        for rc in range(nrc):
            def f(q, rc=rc, dst=dst, src_ap=src_ap, ncols=ncols, wait=wait, ts_gate=ts_gate):
                if rc == 0 and ts_gate:
                    q.wait_ge(p.h("ts"), ts_gate)
                if wait is not None and rc == 0:
                    for s_ in wait:
                        q.wait_ge(p.h(s_[0]), s_[1])
                return q.dma_start(out=dst[:, rc * ncols:(rc + 1) * ncols],
                                   in_=src_ap[rc * 128:(rc + 1) * 128, :])
            sems.append(p.dma("gpsimd", "lw", f))
        return sems

    pa_guard = 0
    def proj_to_hbm(src_hbm, src_rows_chunks, wgrid, wcols, out_hbm, out_rows_off, out_rows,
                    wload_sems, scale=None, extra_ev=None, first_wait=None):
        nonlocal pa_guard
        for ncc in range(NC):
            lds = [rs_x.load(src_hbm[cc * 128:(cc + 1) * 128, ncc * 512:(ncc + 1) * 512],
                             extra_wait=(wload_sems if (ncc == 0 and cc == 0) else None))
                   for cc in range(src_rows_chunks)]

            def t_mm(q, lds=lds, wgrid=wgrid, wcols=wcols, ncc=ncc, first_wait=first_wait,
                     out_rows=out_rows, wpg=pa_guard):
                if wpg:
                    q.wait_ge(p.h("vs"), wpg)
                if first_wait is not None and ncc == 0:
                    for s_ in first_wait:
                        q.wait_ge(p.h(s_[0]), s_[1])
                for cc in range(len(lds)):
                    rs_x.wait(q, lds[cc])
                    last = q.matmul(pa[0:out_rows, :], wgrid(cc), rs_x.ap(lds[cc]),
                                    start=(cc == 0), stop=(cc == len(lds) - 1))
                return last
            ts_mm = p.step("tensor", "ts", t_mm)
            rs_x.set_free(lds, ts_mm)

            ob, oprev = oring.acquire()

            def v_ev(q, w=ts_mm, ob=ob, oprev=oprev, scale=scale, out_rows=out_rows):
                q.wait_ge(p.h("ts"), w)
                if oprev is not None:
                    q.wait_ge(p.h(oprev[0]), oprev[1])
                if scale is not None:
                    return q.tensor_scalar(out=oring.bufs[ob][0:out_rows, :], in0=pa[0:out_rows, :],
                                           scalar1=scale, scalar2=None, op0=ALU.mult)
                return q.tensor_copy(out=oring.bufs[ob][0:out_rows, :], in_=pa[0:out_rows, :])
            vs_ev = p.step("vector", "vs", v_ev)
            pa_guard = vs_ev
            oring.store(ob, out_hbm[out_rows_off:out_rows_off + out_rows,
                                    ncc * 512:(ncc + 1) * 512], vs_ev, rows=out_rows)

    wkv_l = load_wgrid(wst, Wkv_i, 8, KVR)

    sq_guard = 0
    acc_guard = 0
    for rc in range(2):
        for ncc in range(NC):
            lds = [rs_x.load(xnT_s[cc * 128:(cc + 1) * 128, ncc * 512:(ncc + 1) * 512])
                   for cc in range(8)]

            def t_mm(q, lds=lds, rc=rc, wkv_l=wkv_l, first=(rc == 0 and ncc == 0), wsq=sq_guard):
                if wsq:
                    q.wait_ge(p.h("ss"), wsq)
                if first:
                    for s_ in wkv_l:
                        q.wait_ge(p.h(s_[0]), s_[1])
                for cc in range(8):
                    rs_x.wait(q, lds[cc])
                    last = q.matmul(pa[:], wst[:, cc * KVR + rc * 128:cc * KVR + (rc + 1) * 128],
                                    rs_x.ap(lds[cc]), start=(cc == 0), stop=(cc == 7))
                return last
            ts_mm = p.step("tensor", "ts", t_mm)
            rs_x.set_free(lds, ts_mm)

            def s_sq(q, w=ts_mm):
                q.wait_ge(p.h("ts"), w)
                return q.activation(out=sqst[:], in_=pa[:], func=AF.Square, bias=0.0, scale=1.0)
            ss_sq = p.step("scalar", "ss", s_sq)
            sq_guard = ss_sq

            def t_sum(q, w=ss_sq, wacc=acc_guard):
                q.wait_ge(p.h("ss"), w)
                if wacc:
                    q.wait_ge(p.h("vs"), wacc)
                return q.matmul(psm[:], onescf[:], sqst[:], start=True, stop=True)
            ts_sum = p.step("tensor", "ts", t_sum)

            def v_acc(q, ncc=ncc, w=ts_sum):
                q.wait_ge(p.h("ts"), w)
                return q.tensor_add(out=msrow[:, ncc * 512:(ncc + 1) * 512],
                                    in0=msrow[:, ncc * 512:(ncc + 1) * 512], in1=psm[:])
            acc_guard = p.step("vector", "vs", v_acc)

    def s_r2(q, w=p.val("vs")):
        q.wait_ge(p.h("vs"), w)
        q.activation(out=rs2row[:], in_=msrow[:], func=AF.Ln, bias=consts[0:1, 0:1],
                     scale=1.0 / KVR)
        q.memzero(consts[0:1, 3:4])
        return q.activation(out=rs2row[:], in_=rs2row[:], func=AF.Exp, bias=0.0,
                            scale=consts[0:1, 1:2])
    ss_r2 = p.step("scalar", "ss", s_r2)

    evB_guard = 0
    for rc in range(2):
        for ncc in range(NC):
            lds = [rs_x.load(xnT_s[cc * 128:(cc + 1) * 128, ncc * 512:(ncc + 1) * 512])
                   for cc in range(8)]

            def t_mm(q, lds=lds, rc=rc, ncc=ncc, wr2=ss_r2, wev=evB_guard):
                if wev:
                    q.wait_ge(p.h("vs"), wev)
                if rc == 0 and ncc == 0:
                    q.wait_ge(p.h("ss"), wr2)
                for cc in range(8):
                    rs_x.wait(q, lds[cc])
                    q.matmul(pa[:], wst[:, cc * KVR + rc * 128:cc * KVR + (rc + 1) * 128],
                             rs_x.ap(lds[cc]), start=(cc == 0), stop=(cc == 7))
                return q.matmul(pb_[:], onesr[:], rs2row[:, ncc * 512:(ncc + 1) * 512],
                                start=True, stop=True)
            ts_mm = p.step("tensor", "ts", t_mm)
            rs_x.set_free(lds, ts_mm)

            def v_r2b(q, w=ts_mm):
                q.wait_ge(p.h("ts"), w)
                return q.tensor_copy(out=rs2b[:], in_=pb_[:])
            vs_r2b = p.step("vector", "vs", v_r2b)

            ob, oprev = oring.acquire()

            def v_ev(q, rc=rc, ob=ob, oprev=oprev, w=vs_r2b):
                q.wait_ge(p.h("vs"), w)
                if oprev is not None:
                    q.wait_ge(p.h(oprev[0]), oprev[1])
                return q.scalar_tensor_tensor(out=oring.bufs[ob][:], in0=pa[:],
                                              scalar=gkv[rc][:], in1=rs2b[:],
                                              op0=ALU.mult, op1=ALU.mult)
            vs_ev = p.step("vector", "vs", v_ev)
            evB_guard = vs_ev
            oring.store(ob, kvcn_s[rc * 128:(rc + 1) * 128, ncc * 512:(ncc + 1) * 512], vs_ev)

    wq_l = load_wgrid(wst2, Wq_i, 8, QR)
    for rc in range(4):
        proj_to_hbm(xnT_s, 8, lambda cc, rc=rc: wst2[:, cc * QR + rc * 128:cc * QR + (rc + 1) * 128],
                    QR, qc_s, rc * 128, 128, wq_l if rc == 0 else None)

    for h in range(NH):
        wk_l = load_wgrid(wst3, Wkdec_i[:, h * HD:(h + 1) * HD], 2, HD)
        proj_to_hbm(kvcn_s, 2,
                    lambda cc: wst3[:, cc * HD:(cc + 1) * HD],
                    HD, KT_o, h * QKD, HD, wk_l)

    for h in range(NH):
        wk_l = load_wgrid(wst3, Wqdec_i[:, h * HD:(h + 1) * HD], 4, HD)
        proj_to_hbm(qc_s, 4,
                    lambda cc: wst3[:, cc * HD:(cc + 1) * HD],
                    HD, QT_o, h * QKD, HD, wk_l, scale=SCALE)

    wv_l = load_wgrid(wst3, Wvdec_i, 2, D)
    vev_guard = 0
    for jc in range(32):
        lds = [rs_x.load(kvcn_s[rc * 128:(rc + 1) * 128, jc * 128:(jc + 1) * 128])
               for rc in range(2)]

        def t_v(q, lds=lds, jc=jc, wv_l=wv_l, wev=vev_guard):
            if wev:
                q.wait_ge(p.h("vs"), wev)
            if jc == 0:
                for s_ in wv_l:
                    q.wait_ge(p.h(s_[0]), s_[1])
            for rc in range(2):
                rs_x.wait(q, lds[rc])
                last = q.matmul(pa[:, 0:512], lds[rc][0] * 0 + 0 if False else rs_x.ap(lds[rc]),
                                wst3[:, rc * D:rc * D + 512], start=(rc == 0), stop=(rc == 1))
            for rc in range(2):
                last = q.matmul(pb_[:, 0:512], rs_x.ap(lds[rc]),
                                wst3[:, rc * D + 512:rc * D + 1024], start=(rc == 0), stop=(rc == 1))
            return last
        ts_v = p.step("tensor", "ts", t_v)
        rs_x.set_free(lds, ts_v)

        ob, oprev = oring.acquire()
        ob2, oprev2 = oring.acquire()

        def v_ev(q, w=ts_v, ob=ob, oprev=oprev, ob2=ob2, oprev2=oprev2):
            q.wait_ge(p.h("ts"), w)
            if oprev is not None:
                q.wait_ge(p.h(oprev[0]), oprev[1])
            if oprev2 is not None:
                q.wait_ge(p.h(oprev2[0]), oprev2[1])
            q.tensor_copy(out=oring.bufs[ob][:], in_=pa[:, 0:512])
            return q.tensor_copy(out=oring.bufs[ob2][:], in_=pb_[:, 0:512])
        vs_ev = p.step("vector", "vs", v_ev)
        vev_guard = vs_ev
        oring.store(ob, V_o[jc * 128:(jc + 1) * 128, 0:512], vs_ev)
        oring.store(ob2, V_o[jc * 128:(jc + 1) * 128, 512:1024], vs_ev)

    def rope_phase(src_hbm, nsrc, Wr_i, out_hbm, scale):
        wl = load_wgrid(wrp, Wr_i, nsrc, NH * RD)
        def v_rot(q, wl=wl, nsrc=nsrc):
            for s_ in wl:
                q.wait_ge(p.h(s_[0]), s_[1])
            for cc in range(nsrc):
                for h in range(NH):
                    base = cc * NH * RD + h * RD
                    q.tensor_scalar(out=wrp2[:, base:base + 16],
                                    in0=wrp[:, base + 16:base + 32],
                                    scalar1=-1.0, scalar2=None, op0=ALU.mult)
                    q.tensor_copy(out=wrp2[:, base + 16:base + 32],
                                  in_=wrp[:, base:base + 16])
            return q.memset(consts[:, 2:3], 0.0)
        vs_rot = p.step("vector", "vs", v_rot)

        rope_guard = 0
        for h in range(NH):
            for ncc in range(NC):
                lds = [rs_x.load(src_hbm[cc * 128:(cc + 1) * 128, ncc * 512:(ncc + 1) * 512])
                       for cc in range(nsrc)]

                def t_mm(q, lds=lds, h=h, ncc=ncc, w=vs_rot, nsrc=nsrc, wrg=rope_guard):
                    if wrg:
                        q.wait_ge(p.h("vs"), wrg)
                    if h == 0 and ncc == 0:
                        q.wait_ge(p.h("vs"), w)
                    for cc in range(nsrc):
                        rs_x.wait(q, lds[cc])
                        q.matmul(pa[0:RD, :], wrp[:, cc * NH * RD + h * RD:cc * NH * RD + (h + 1) * RD],
                                 rs_x.ap(lds[cc]), start=(cc == 0), stop=(cc == nsrc - 1))
                    for cc in range(nsrc):
                        last = q.matmul(pb_[0:RD, :], wrp2[:, cc * NH * RD + h * RD:cc * NH * RD + (h + 1) * RD],
                                        rs_x.ap(lds[cc]), start=(cc == 0), stop=(cc == nsrc - 1))
                    return last
                ts_mm = p.step("tensor", "ts", t_mm)
                rs_x.set_free(lds, ts_mm)

                ob, oprev = oring.acquire()

                def v_rope(q, ncc=ncc, w=ts_mm, ob=ob, oprev=oprev, scale=scale):
                    q.wait_ge(p.h("ts"), w)
                    if oprev is not None:
                        q.wait_ge(p.h(oprev[0]), oprev[1])
                    sl = slice(ncc * 512, (ncc + 1) * 512)
                    q.tensor_tensor(out=krp[:, sl], in0=pa[0:RD, :], in1=cosb[:, sl], op=ALU.mult)
                    q.tensor_tensor(out=krr[:, sl], in0=pb_[0:RD, :], in1=sinb[:, sl], op=ALU.mult)
                    if scale is None:
                        return q.tensor_add(out=oring.bufs[ob][0:RD, :], in0=krp[:, sl], in1=krr[:, sl])
                    q.tensor_add(out=krt2[:, sl], in0=krp[:, sl], in1=krr[:, sl])
                    return q.tensor_scalar(out=oring.bufs[ob][0:RD, :], in0=krt2[:, sl],
                                           scalar1=scale, scalar2=None, op0=ALU.mult)
                vs_rope = p.step("vector", "vs", v_rope)
                rope_guard = vs_rope
                oring.store(ob, out_hbm[h * QKD + HD:h * QKD + HD + RD, ncc * 512:(ncc + 1) * 512],
                            vs_rope, rows=RD)

    rope_phase(xnT_s, 8, Wkrope_i, KT_o, None)
    rope_phase(qc_s, 4, Wqrope_i, QT_o, SCALE)

    so_t = p.pool_totals("so")
    lx_t = p.pool_totals("lx")
    lw_t = p.pool_totals("lw")
    def fin(q):
        p.emit_pool_wait(q, so_t)
        p.emit_pool_wait(q, lx_t)
        p.emit_pool_wait(q, lw_t)
    p.on("gpsimd", fin)
    nc = p.build()
    return p, nc





NH, HD = 16, 64
QKD = 96


def build_E2b(core_id):
    p = Prog()
    QT_i = p.din("QT", [NH * QKD, T], BF16)
    KT_i = p.din("KT", [NH * QKD, T], BF16)
    V_i = p.din("V", [T, D], BF16)
    maskt_i = p.din("maskt", [4, 128, 512], F32)
    OT_o = p.dout("OT", [D, T], BF16)

    Qt = [p.sb(f"qte{i}", [QKD, T], BF16) for i in range(2)]
    Kt = [p.sb(f"kte{i}", [QKD, T], BF16) for i in range(2)]
    Vh = [p.sb(f"vhe{i}", [128, 32 * HD], BF16) for i in range(2)]
    maskt = p.sb("maskte", [128, 4 * 512], F32)
    WTc = [p.sb(f"wtce{i}", [128, 512], BF16) for i in range(4)]
    wsub = [p.sb(f"wsube{i}", [128, 512], F32) for i in range(2)]
    rrs = p.sb("rrse1", [1, 512], F32)
    rrsb = p.sb("rrsbe1", [64, 512], F32)
    ones64 = p.sb("ones64e", [1, 64], F32)
    onescb = p.sb("onescbe", [128, 1], BF16)

    psT = [p.ps(f"psTe{i}", [128, 512], F32) for i in range(4)]
    pav = p.ps("pave", [64, 512], F32)
    psm = p.ps("psme", [1, 512], F32)
    pmb = p.ps("pmbe", [64, 512], F32)

    for nm in ["vs", "ss", "ts"]:
        p.sem(nm)
    p.dma_pool("lk", 4)
    p.dma_pool("lv", 4)
    p.dma_pool("so", 4)

    oev = [p.sb(f"oeve{i}", [64, 512], BF16) for i in range(3)]
    ostore_prev = {}

    cl = [p.dma("sync", "lv", lambda q, a=a: q.dma_start(
        out=maskt[:, a * 512:(a + 1) * 512], in_=maskt_i[a, :, :])) for a in range(4)]

    def vinit(q):
        for s_ in cl:
            q.wait_ge(p.h(s_[0]), s_[1])
        q.memset(ones64[:], 1.0)
        return q.memset(onescb[:], 1.0)
    vs_init = p.step("vector", "vs", vinit)

    head_end = [0, 0]
    psT_guard = [0, 0]
    wtc_guard = [0, 0]
    pav_guard = vs_init
    ev_i = 0

    for h in range(NH):
        b = h % 2
        lds = []
        for i in range(8):
            def fk(q, h=h, b=b, i=i, w=head_end[b]):
                if w and i == 0:
                    q.wait_ge(p.h("ts"), w)
                return q.dma_start(out=Kt[b][:, i * 512:(i + 1) * 512],
                                   in_=KT_i[h * QKD:(h + 1) * QKD, i * 512:(i + 1) * 512])
            lds.append(p.dma("gpsimd", "lk", fk))
        for i in range(8):
            def fq_(q, h=h, b=b, i=i):
                return q.dma_start(out=Qt[b][:, i * 512:(i + 1) * 512],
                                   in_=QT_i[h * QKD:(h + 1) * QKD, i * 512:(i + 1) * 512])
            lds.append(p.dma("gpsimd", "lk", fq_))
        for jt in range(32):
            def fv(q, h=h, b=b, jt=jt, w=head_end[b]):
                if w and jt == 0:
                    q.wait_ge(p.h("ts"), w)
                return q.dma_start(out=Vh[b][:, jt * HD:(jt + 1) * HD],
                                   in_=V_i[jt * 128:(jt + 1) * 128, h * 64:(h + 1) * 64])
            lds.append(p.dma("sync", "lv", fv))

        last_ts = 0
        for qc in range(8):
            njt = 4 * qc + 4
            for jp in range(njt // 2):
                pb = jp % 2
                i0, i1 = pb * 2, pb * 2 + 1
                jt0 = 2 * jp
                isdiag = jt0 >= 4 * qc

                def t_s(q, b=b, qc=qc, jt0=jt0, i0=i0, i1=i1, lds=lds, wfree=psT_guard[pb],
                        first=(qc == 0 and jp == 0)):
                    if first:
                        for s_ in lds:
                            q.wait_ge(p.h(s_[0]), s_[1])
                    if wfree:
                        q.wait_ge(p.h("ss"), wfree)
                    q.matmul(psT[i0][:], Kt[b][:, jt0 * 128:(jt0 + 1) * 128],
                             Qt[b][:, qc * 512:(qc + 1) * 512], start=True, stop=True)
                    return q.matmul(psT[i1][:], Kt[b][:, (jt0 + 1) * 128:(jt0 + 2) * 128],
                                    Qt[b][:, qc * 512:(qc + 1) * 512], start=True, stop=True)
                ts_s = p.step("tensor", "ts", t_s)

                if isdiag:
                    def v_sub(q, qc=qc, jt0=jt0, i0=i0, i1=i1, w=ts_s):
                        q.wait_ge(p.h("ts"), w)
                        v0 = jt0 - 4 * qc
                        q.tensor_tensor(out=wsub[0][:], in0=psT[i0][:],
                                        in1=maskt[:, v0 * 512:(v0 + 1) * 512], op=ALU.add)
                        return q.tensor_tensor(out=wsub[1][:], in0=psT[i1][:],
                                               in1=maskt[:, (v0 + 1) * 512:(v0 + 2) * 512],
                                               op=ALU.add)
                    vs_sub = p.step("vector", "vs", v_sub)

                    def s_exp(q, i0=i0, i1=i1, w=vs_sub, wav=wtc_guard[pb]):
                        q.wait_ge(p.h("vs"), w)
                        if wav:
                            q.wait_ge(p.h("ts"), wav)
                        q.activation(out=WTc[i0][:], in_=wsub[0][:], func=AF.Exp,
                                     bias=0.0, scale=1.0)
                        return q.activation(out=WTc[i1][:], in_=wsub[1][:], func=AF.Exp,
                                            bias=0.0, scale=1.0)
                    ss_exp = p.step("scalar", "ss", s_exp)
                else:
                    def s_exp(q, i0=i0, i1=i1, w=ts_s, wav=wtc_guard[pb]):
                        q.wait_ge(p.h("ts"), w)
                        if wav:
                            q.wait_ge(p.h("ts"), wav)
                        q.activation(out=WTc[i0][:], in_=psT[i0][:], func=AF.Exp,
                                     bias=0.0, scale=1.0)
                        return q.activation(out=WTc[i1][:], in_=psT[i1][:], func=AF.Exp,
                                            bias=0.0, scale=1.0)
                    ss_exp = p.step("scalar", "ss", s_exp)
                psT_guard[pb] = ss_exp

                def t_av(q, b=b, qc=qc, jp=jp, jt0=jt0, i0=i0, i1=i1, w=ss_exp,
                         first=(jp == 0), wpav=pav_guard):
                    q.wait_ge(p.h("ss"), w)
                    if first:
                        q.wait_ge(p.h("vs"), wpav)
                    laststop = (jt0 + 1 == 4 * qc + 3)
                    q.matmul(pav[:], Vh[b][:, jt0 * HD:(jt0 + 1) * HD], WTc[i0][:],
                             start=first, stop=False)
                    q.matmul(pav[:], Vh[b][:, (jt0 + 1) * HD:(jt0 + 2) * HD], WTc[i1][:],
                             start=False, stop=laststop)
                    q.matmul(psm[:], onescb[:], WTc[i0][:], start=first, stop=False)
                    return q.matmul(psm[:], onescb[:], WTc[i1][:], start=False, stop=laststop)
                ts_av = p.step("tensor", "ts", t_av)
                wtc_guard[pb] = ts_av
                last_ts = ts_av

            def v_f1(q, w=last_ts):
                q.wait_ge(p.h("ts"), w)
                return q.reciprocal(out=rrs[:], in_=psm[:])
            vs_f1 = p.step("vector", "vs", v_f1)

            def t_rb(q, w=vs_f1):
                q.wait_ge(p.h("vs"), w)
                return q.matmul(pmb[0:64, :], ones64[:], rrs[:], start=True, stop=True)
            ts_rb = p.step("tensor", "ts", t_rb)

            ob = ev_i % 3
            ev_i += 1

            def v_f2(q, ob=ob, w=ts_rb, wso=ostore_prev.get(ob)):
                q.wait_ge(p.h("ts"), w)
                if wso is not None:
                    q.wait_ge(p.h(wso[0]), wso[1])
                q.tensor_copy(out=rrsb[:], in_=pmb[0:64, :])
                return q.tensor_tensor(out=oev[ob][:], in0=pav[:], in1=rrsb[:], op=ALU.mult)
            vs_f2 = p.step("vector", "vs", v_f2)
            pav_guard = vs_f2

            def st_o(q, h=h, qc=qc, ob=ob, w=vs_f2):
                q.wait_ge(p.h("vs"), w)
                return q.dma_start(out=OT_o[h * 64:(h + 1) * 64, qc * 512:(qc + 1) * 512],
                                   in_=oev[ob][:])
            ostore_prev[ob] = p.dma("sync", "so", st_o)
        head_end[b] = last_ts

    so_t = p.pool_totals("so")
    p.on("gpsimd", lambda q: p.emit_pool_wait(q, so_t))
    nc = p.build()
    return p, nc






def build_E2c(core_id):
    p = Prog()
    xt_i = p.din("xt_own", [D, T], F32)
    OT_i = p.din("OT", [D, T], BF16)
    Wo_i = p.din("Wo", [D, D], F32)
    ffn0_i = p.din("ffn0", [T, D], F32)
    ident_i = p.din("ident", [128, 128], F32)
    x1T_o = p.dout("x1T", [D, T], F32)

    wst = p.sb("wstc", [128, 1024], BF16)
    ident = p.sb("identc", [128, 128], F32)
    f0st = p.sb("f0st", [128, 512], F32)
    f0T = p.sb("f0Tc", [128, 512], F32)
    xadd = p.sb("xaddc", [128, 512], F32)

    rs_o = RhsStream(p, "rso", [128, 512], BF16, n=10, pool="lx", gate_sem="ts")
    oring = OutRing(p, "orngc", [128, 512], F32, n=3, pool="so", prod_sem="vs")

    pa = p.ps("pac", [128, 512], F32)
    ptr = p.ps("ptrc", [128, 128], F32)

    for nm in ["vs", "ss", "ts"]:
        p.sem(nm)
    p.dma_pool("ld", 4)
    p.dma_pool("lx", 4)
    p.dma_pool("lw", 2)
    p.dma_pool("so", 4)

    cid = p.dma("sync", "ld", lambda q: q.dma_start(out=ident[:], in_=ident_i[:, :]))
    pa_guard = 0
    f0_guard = 0

    for oc in range(8):
        ts_gate = p.val("ts")
        wl = []
        for cc in range(8):
            def f(q, cc=cc, oc=oc, ts_gate=ts_gate):
                if cc == 0 and ts_gate:
                    q.wait_ge(p.h("ts"), ts_gate)
                return q.dma_start(out=wst[:, cc * 128:(cc + 1) * 128],
                                   in_=Wo_i[cc * 128:(cc + 1) * 128, oc * 128:(oc + 1) * 128])
            wl.append(p.dma("gpsimd", "lw", f))

        for ncc in range(8):
            lds = [rs_o.load(OT_i[cc * 128:(cc + 1) * 128, ncc * 512:(ncc + 1) * 512])
                   for cc in range(8)]
            lf = []
            for a in range(4):
                lf.append(p.dma("sync", "ld", lambda q, oc=oc, ncc=ncc, a=a, w=f0_guard:
                          ((q.wait_ge(p.h("ts"), w) if (w and a == 0) else None),
                           q.dma_start(out=f0st[:, a * 128:(a + 1) * 128],
                                       in_=ffn0_i[ncc * 512 + a * 128:ncc * 512 + (a + 1) * 128,
                                                  oc * 128:(oc + 1) * 128]))[-1]))
            lx = p.dma("sync", "ld", lambda q, oc=oc, ncc=ncc:
                       q.dma_start(out=xadd[:], in_=xt_i[oc * 128:(oc + 1) * 128,
                                                         ncc * 512:(ncc + 1) * 512]))

            def t_mm(q, lds=lds, wl=wl, ncc=ncc, wpg=pa_guard, cid=cid):
                if wpg:
                    q.wait_ge(p.h("vs"), wpg)
                if ncc == 0:
                    for s_ in wl:
                        q.wait_ge(p.h(s_[0]), s_[1])
                    q.wait_ge(p.h(cid[0]), cid[1])
                for cc in range(8):
                    rs_o.wait(q, lds[cc])
                    last = q.matmul(pa[:], wst[:, cc * 128:(cc + 1) * 128], rs_o.ap(lds[cc]),
                                    start=(cc == 0), stop=(cc == 7))
                return last
            ts_mm = p.step("tensor", "ts", t_mm)
            rs_o.set_free(lds, ts_mm)

            vprev = 0
            for a in range(4):
                def t_tr(q, a=a, lf=lf, w=vprev):
                    q.wait_ge(p.h(lf[a][0]), lf[a][1])
                    if w:
                        q.wait_ge(p.h("vs"), w)
                    return q.transpose(ptr[:], f0st[:, a * 128:(a + 1) * 128], ident[:])
                ts_tr = p.step("tensor", "ts", t_tr)

                def v_tr(q, a=a, w=ts_tr):
                    q.wait_ge(p.h("ts"), w)
                    return q.tensor_copy(out=f0T[:, a * 128:(a + 1) * 128], in_=ptr[:])
                vprev = p.step("vector", "vs", v_tr)
            f0_guard = ts_tr

            ob, oprev = oring.acquire()

            def v_fin(q, lx=lx, w1=ts_mm, w2=vprev, ob=ob, oprev=oprev):
                q.wait_ge(p.h("ts"), w1)
                q.wait_ge(p.h("vs"), w2)
                q.wait_ge(p.h(lx[0]), lx[1])
                if oprev is not None:
                    q.wait_ge(p.h(oprev[0]), oprev[1])
                q.tensor_add(out=xadd[:], in0=xadd[:], in1=pa[:])
                return q.tensor_add(out=oring.bufs[ob][:], in0=xadd[:], in1=f0T[:])
            vs_fin = p.step("vector", "vs", v_fin)
            pa_guard = vs_fin
            oring.store(ob, x1T_o[oc * 128:(oc + 1) * 128, ncc * 512:(ncc + 1) * 512], vs_fin)

    so_t = p.pool_totals("so")
    p.on("gpsimd", lambda q: p.emit_pool_wait(q, so_t))
    nc = p.build()
    return p, nc






def build_F0(core_id):
    p = Prog()
    x1T_i = p.din("x1T", [D, T], F32)
    stats_o = p.dout("stats1", [2, T], F32)
    xnf_o = p.dout("xn1Tf", [D, T], F32)
    xnb_o = p.dout("xn1Tb", [D, T], BF16)

    xst = p.sb("xst", [128, 4096], F32)
    sq = [p.sb(f"sqf{i}", [128, 512], F32) for i in range(2)]
    murow = p.sb("murowf", [1, T], F32)
    rsrow = p.sb("rsrowf", [1, T], F32)
    m2row = p.sb("m2rowf", [1, 512], F32)
    varrow = p.sb("varrowf", [1, 512], F32)
    rsb = p.sb("rsbf", [128, 512], F32)
    mub = p.sb("mubf", [128, 512], F32)
    xsub = p.sb("xsubf", [128, 512], F32)
    onesr = p.sb("onesrf", [1, 128], F32)
    onescf = p.sb("onescff", [128, 1], F32)
    consts = p.sb("constsf", [1, 4], F32)

    ps_s = p.ps("pss", [1, 512], F32)
    ps_q = p.ps("psq", [1, 512], F32)
    pa = p.ps("paf", [128, 512], F32)
    pb_ = p.ps("pbf", [128, 512], F32)

    for nm in ["vs", "ss", "ts"]:
        p.sem(nm)
    p.dma_pool("lx", 4)
    p.dma_pool("so", 6)

    orf = OutRing(p, "orff", [128, 512], F32, n=3, pool="so", prod_sem="vs")
    orb = OutRing(p, "orbf", [128, 512], BF16, n=3, pool="so", prod_sem="ss")

    def vinit(q):
        q.memset(onesr[:], 1.0)
        q.memset(onescf[:], 1.0)
        q.memset(consts[:, 0:1], 1e-5)
        return q.memset(consts[:, 1:2], -0.5)
    vs_init = p.step("vector", "vs", vinit)

    xgate = [0] * 8
    sq_free = [0, 0]
    row_read = 0
    bc_read = vs_init

    for ncc in range(8):
        lds = []
        for cc in range(8):
            def f(q, cc=cc, ncc=ncc, w=xgate[cc]):
                if w:
                    q.wait_ge(p.h("vs"), w)
                return q.dma_start(out=xst[:, cc * 512:(cc + 1) * 512],
                                   in_=x1T_i[cc * 128:(cc + 1) * 128,
                                             ncc * 512:(ncc + 1) * 512])
            lds.append(p.dma("sync", "lx", f))

        ts_mm = None
        for cc in range(8):
            b = cc % 2

            def s_sq(q, cc=cc, b=b, lds=lds, wfree=sq_free[b]):
                q.wait_ge(p.h(lds[cc][0]), lds[cc][1])
                if wfree:
                    q.wait_ge(p.h("ts"), wfree)
                return q.activation(out=sq[b][:], in_=xst[:, cc * 512:(cc + 1) * 512],
                                    func=AF.Square, bias=0.0, scale=1.0)
            ss_sq = p.step("scalar", "ss", s_sq)

            def t_mm(q, cc=cc, b=b, lds=lds, wsq=ss_sq, wrr=row_read, vi=vs_init):
                if cc == 0:
                    q.wait_ge(p.h("vs"), max(wrr, vi))
                q.wait_ge(p.h(lds[cc][0]), lds[cc][1])
                q.matmul(ps_s[:], onescf[:], xst[:, cc * 512:(cc + 1) * 512],
                         start=(cc == 0), stop=(cc == 7))
                q.wait_ge(p.h("ss"), wsq)
                return q.matmul(ps_q[:], onescf[:], sq[b][:],
                                start=(cc == 0), stop=(cc == 7))
            ts_mm = p.step("tensor", "ts", t_mm)
            sq_free[b] = ts_mm

        def v_row(q, ncc=ncc, w=ts_mm):
            q.wait_ge(p.h("ts"), w)
            q.tensor_scalar(out=murow[:, ncc * 512:(ncc + 1) * 512], in0=ps_s[:],
                            scalar1=1.0 / 1024.0, scalar2=None, op0=ALU.mult)
            q.tensor_tensor(out=m2row[:], in0=murow[:, ncc * 512:(ncc + 1) * 512],
                            in1=murow[:, ncc * 512:(ncc + 1) * 512], op=ALU.mult)
            return q.scalar_tensor_tensor(out=varrow[:], in0=ps_q[:], scalar=1.0 / 1024.0,
                                          in1=m2row[:], op0=ALU.mult, op1=ALU.subtract)
        vs_row = p.step("vector", "vs", v_row)
        row_read = vs_row

        def s_rstd(q, ncc=ncc, w=vs_row):
            q.wait_ge(p.h("vs"), w)
            q.activation(out=varrow[:], in_=varrow[:], func=AF.Ln,
                         bias=consts[0:1, 0:1], scale=1.0)
            return q.activation(out=rsrow[:, ncc * 512:(ncc + 1) * 512], in_=varrow[:],
                                func=AF.Exp, bias=0.0, scale=consts[0:1, 1:2])
        ss_rstd = p.step("scalar", "ss", s_rstd)

        def t_bc(q, ncc=ncc, w=ss_rstd, wbc=bc_read):
            q.wait_ge(p.h("ss"), w)
            q.wait_ge(p.h("vs"), wbc)
            q.matmul(pa[:], onesr[:], rsrow[:, ncc * 512:(ncc + 1) * 512], start=True, stop=True)
            return q.matmul(pb_[:], onesr[:], murow[:, ncc * 512:(ncc + 1) * 512], start=True, stop=True)
        ts_bc = p.step("tensor", "ts", t_bc)

        def v_bc(q, w=ts_bc):
            q.wait_ge(p.h("ts"), w)
            q.tensor_copy(out=rsb[:], in_=pa[:])
            return q.tensor_copy(out=mub[:], in_=pb_[:])
        vs_bc = p.step("vector", "vs", v_bc)
        bc_read = vs_bc

        for cc in range(8):
            ob, oprev = orf.acquire()

            def v_xn(q, cc=cc, w=vs_bc, ob=ob, oprev=oprev):
                q.wait_ge(p.h("vs"), w)
                if oprev is not None:
                    q.wait_ge(p.h(oprev[0]), oprev[1])
                q.tensor_sub(out=xsub[:], in0=xst[:, cc * 512:(cc + 1) * 512], in1=mub[:])
                return q.tensor_tensor(out=orf.bufs[ob][:], in0=xsub[:], in1=rsb[:], op=ALU.mult)
            vs_xn = p.step("vector", "vs", v_xn)
            xgate[cc] = vs_xn
            orf.store(ob, xnf_o[cc * 128:(cc + 1) * 128, ncc * 512:(ncc + 1) * 512], vs_xn)

            ob2, oprev2 = orb.acquire()

            def s_bf(q, w=vs_xn, ob=ob, ob2=ob2, oprev2=oprev2):
                q.wait_ge(p.h("vs"), w)
                if oprev2 is not None:
                    q.wait_ge(p.h(oprev2[0]), oprev2[1])
                return q.activation(out=orb.bufs[ob2][:], in_=orf.bufs[ob][:],
                                    func=AF.Copy, bias=0.0, scale=1.0)
            ss_bf = p.step("scalar", "ss", s_bf)
            orb.store(ob2, xnb_o[cc * 128:(cc + 1) * 128, ncc * 512:(ncc + 1) * 512], ss_bf)

    def st_mu(q):
        q.wait_ge(p.h("vs"), p.val("vs"))
        return q.dma_start(out=stats_o[0:1, :], in_=murow[:])
    p.dma("sync", "so", st_mu)

    def st_rs(q):
        q.wait_ge(p.h("ss"), p.val("ss"))
        return q.dma_start(out=stats_o[1:2, :], in_=rsrow[:])
    p.dma("sync", "so", st_rs)

    so_t = p.pool_totals("so")
    p.on("gpsimd", lambda q: p.emit_pool_wait(q, so_t))
    nc = p.build()
    return p, nc





NH, HD = 16, 64
SCALE8 = 0.125


def build_F1a(core_id):
    p = Prog()
    xnf_i = p.din("xn1Tf", [D, T], F32)
    qkvT_i = p.din("qkvT", [D, 3 * D], F32)
    w1_i = p.din("w1c", [128, 8], F32)
    cos_i = p.din("cosT64", [128, T], F32)
    sin_i = p.din("sinT64", [128, T], F32)

    xrT_s = p.dout("xrT", [D, T], BF16)
    QrT_o = p.dout("QrT", [D, T], BF16)
    KrT_o = p.dout("KrT", [D, T], BF16)
    V_o = p.dout("V", [T, D], BF16)

    xst = p.sb("xst1", [128, 4096], F32)
    sq = [p.sb(f"sq1{i}", [128, 512], F32) for i in range(2)]
    rsr_row = p.sb("rsrrow", [1, T], F32)
    rsb = p.sb("rsb1", [128, 512], F32)
    w1st = p.sb("w1st", [128, 8], F32)
    consts = p.sb("consts1", [1, 4], F32)
    onesr = p.sb("onesr1", [1, 128], F32)
    onescf = p.sb("onescf1", [128, 1], F32)
    xmul = p.sb("xmul1", [128, 512], F32)
    xmul2 = [p.sb(f"xmul2{i}", [128, 512], F32) for i in range(2)]
    cosb = p.sb("cosb1", [128, T], F32)
    sinb = p.sb("sinb1", [128, T], F32)
    wst = [p.sb(f"wst1{i}", [128, 1024], BF16) for i in range(2)]
    wrot = [p.sb(f"wrot1{i}", [128, 1024], BF16) for i in range(2)]
    wstv = p.sb("wstv", [128, 4096], BF16)
    qtile = p.sb("qtile", [128, 512], F32)
    rtile = p.sb("rtile", [128, 512], F32)
    xrst = [p.sb(f"xrst{i}", [128, 128], BF16) for i in range(8)]
    scl128 = p.sb("scl128", [128, 1], F32)

    ps_q = p.ps("psq1", [1, 512], F32)
    pa = p.ps("pa1", [128, 512], F32)
    pb_ = p.ps("pb1", [128, 512], F32)

    for nm in ["vs", "ss", "ts"]:
        p.sem(nm)
    p.dma_pool("lx", 4)
    p.dma_pool("lw", 4)
    p.dma_pool("so", 6)

    rs_x = RhsStream(p, "rsx1", [128, 512], BF16, n=10, pool="lx", gate_sem="ts")
    oring = OutRing(p, "orng1", [128, 512], BF16, n=4, pool="so", prod_sem="ss")

    cl = []
    cl.append(p.dma("gpsimd", "lw", lambda q: q.dma_start(out=cosb[:], in_=cos_i[:, :])))
    cl.append(p.dma("gpsimd", "lw", lambda q: q.dma_start(out=sinb[:], in_=sin_i[:, :])))
    cl.append(p.dma("gpsimd", "lw", lambda q: q.dma_start(out=w1st[:], in_=w1_i[:, :])))

    def vinit(q):
        q.memset(onesr[:], 1.0)
        q.memset(onescf[:], 1.0)
        q.memset(consts[:, 0:1], 1e-6)
        q.memset(consts[:, 1:2], -0.5)
        q.memset(scl128[:], SCALE8)
        for s_ in cl:
            q.wait_ge(p.h(s_[0]), s_[1])
        return q.memset(consts[:, 2:3], 0.0)
    vs_init = p.step("vector", "vs", vinit)

    xgate = [0] * 8
    sq_free = [0, 0]
    sc_gate = [0, 0]
    row_read = 0
    bc_read = vs_init
    for ncc in range(8):
        lds = []
        for cc in range(8):
            def f(q, cc=cc, ncc=ncc, w=xgate[cc]):
                if w:
                    q.wait_ge(p.h("ss"), w)
                return q.dma_start(out=xst[:, cc * 512:(cc + 1) * 512],
                                   in_=xnf_i[cc * 128:(cc + 1) * 128,
                                             ncc * 512:(ncc + 1) * 512])
            lds.append(p.dma("sync", "lx", f))

        ts_mm = None
        for cc in range(8):
            b = cc % 2

            def s_sq(q, cc=cc, b=b, lds=lds, wfree=sq_free[b]):
                q.wait_ge(p.h(lds[cc][0]), lds[cc][1])
                if wfree:
                    q.wait_ge(p.h("ts"), wfree)
                return q.activation(out=sq[b][:], in_=xst[:, cc * 512:(cc + 1) * 512],
                                    func=AF.Square, bias=0.0, scale=1.0)
            ss_sq = p.step("scalar", "ss", s_sq)

            def t_mm(q, cc=cc, b=b, wsq=ss_sq, wrr=row_read, vi=vs_init):
                if cc == 0:
                    q.wait_ge(p.h("ss"), wsq)
                    q.wait_ge(p.h("vs"), max(wrr, vi) if wrr else vi)
                else:
                    q.wait_ge(p.h("ss"), wsq)
                return q.matmul(ps_q[:], onescf[:], sq[b][:],
                                start=(cc == 0), stop=(cc == 7))
            ts_mm = p.step("tensor", "ts", t_mm)
            sq_free[b] = ts_mm

        def v_row(q, ncc=ncc, w=ts_mm):
            q.wait_ge(p.h("ts"), w)
            return q.tensor_scalar(out=rsr_row[:, ncc * 512:(ncc + 1) * 512], in0=ps_q[:],
                                   scalar1=1.0 / 1024.0, scalar2=None, op0=ALU.mult)
        vs_row = p.step("vector", "vs", v_row)
        row_read = vs_row

        def s_rstd(q, ncc=ncc, w=vs_row):
            q.wait_ge(p.h("vs"), w)
            q.activation(out=rsr_row[:, ncc * 512:(ncc + 1) * 512],
                         in_=rsr_row[:, ncc * 512:(ncc + 1) * 512], func=AF.Ln,
                         bias=consts[0:1, 0:1], scale=1.0)
            return q.activation(out=rsr_row[:, ncc * 512:(ncc + 1) * 512],
                                in_=rsr_row[:, ncc * 512:(ncc + 1) * 512],
                                func=AF.Exp, bias=0.0, scale=consts[0:1, 1:2])
        ss_rstd = p.step("scalar", "ss", s_rstd)

        def t_bc(q, ncc=ncc, w=ss_rstd, wbc=bc_read):
            q.wait_ge(p.h("ss"), w)
            q.wait_ge(p.h("vs"), wbc)
            return q.matmul(pa[:], onesr[:], rsr_row[:, ncc * 512:(ncc + 1) * 512],
                            start=True, stop=True)
        ts_bc = p.step("tensor", "ts", t_bc)

        def v_bc(q, w=ts_bc):
            q.wait_ge(p.h("ts"), w)
            return q.tensor_copy(out=rsb[:], in_=pa[:])
        vs_bc = p.step("vector", "vs", v_bc)
        bc_read = vs_bc

        for cc in range(8):
            b2 = cc % 2

            def v_xr(q, cc=cc, b2=b2, w=vs_bc, wsc=sc_gate[b2]):
                q.wait_ge(p.h("vs"), w)
                if wsc:
                    q.wait_ge(p.h("ss"), wsc)
                return q.tensor_tensor(out=xmul2[b2][:], in0=xst[:, cc * 512:(cc + 1) * 512],
                                       in1=rsb[:], op=ALU.mult)
            vs_xr = p.step("vector", "vs", v_xr)

            ob, oprev = oring.acquire()

            def s_cast(q, cc=cc, b2=b2, w=vs_xr, ob=ob, oprev=oprev):
                q.wait_ge(p.h("vs"), w)
                if oprev is not None:
                    q.wait_ge(p.h(oprev[0]), oprev[1])
                return q.activation(out=oring.bufs[ob][:], in_=xmul2[b2][:], func=AF.Copy,
                                    bias=0.0, scale=w1st[:, cc:cc + 1])
            ss_cast = p.step("scalar", "ss", s_cast)
            sc_gate[b2] = ss_cast
            xgate[cc] = ss_cast
            oring.store(ob, xrT_s[cc * 128:(cc + 1) * 128, ncc * 512:(ncc + 1) * 512], ss_cast)

    xr_done = p.pool_totals("so")
    xr_done_v = dict(xr_done)

    wg_free = [0, 0]
    pa_free = 0
    rt_free = 0
    first_qk = True
    for dt in range(16):
        wb = dt % 2
        wl = []
        for cc in range(8):
            def f(q, cc=cc, dt=dt, wb=wb, wfree=wg_free[wb], first=(cc == 0), fq=first_qk):
                if fq and cc == 0:
                    p.emit_pool_wait(q, xr_done_v)
                if wfree and cc == 0:
                    q.wait_ge(p.h("ts"), wfree)
                return q.dma_start(out=wst[wb][:, cc * 128:(cc + 1) * 128],
                                   in_=qkvT_i[cc * 128:(cc + 1) * 128,
                                              dt * 128:(dt + 1) * 128])
            wl.append(p.dma("gpsimd", "lw", f))
        first_qk = False

        def v_wrot(q, wl=wl, wb=wb, wfree=wg_free[wb]):
            if wfree:
                q.wait_ge(p.h("ts"), wfree)
            for s_ in wl:
                q.wait_ge(p.h(s_[0]), s_[1])
            for blk in range(8):
                a0 = blk * 128
                for a in (a0, a0 + 64):
                    q.tensor_scalar(out=wrot[wb][:, a:a + 32], in0=wst[wb][:, a + 32:a + 64],
                                    scalar1=-1.0, scalar2=None, op0=ALU.mult)
                    last = q.tensor_copy(out=wrot[wb][:, a + 32:a + 64], in_=wst[wb][:, a:a + 32])
            return last
        vs_wrot = p.step("vector", "vs", v_wrot)

        for ncc in range(8):
            lds = [rs_x.load(xrT_s[cc * 128:(cc + 1) * 128, ncc * 512:(ncc + 1) * 512])
                   for cc in range(8)]

            def t_mm(q, lds=lds, wb=wb, ncc=ncc, wpa=pa_free, wr=vs_wrot):
                if wpa:
                    q.wait_ge(p.h("vs"), wpa)
                if ncc == 0:
                    q.wait_ge(p.h("vs"), wr)
                for cc in range(8):
                    rs_x.wait(q, lds[cc])
                    q.matmul(pa[:], wst[wb][:, cc * 128:(cc + 1) * 128],
                             rs_x.ap(lds[cc]), start=(cc == 0), stop=(cc == 7))
                for cc in range(8):
                    last = q.matmul(pb_[:], wrot[wb][:, cc * 128:(cc + 1) * 128],
                                    rs_x.ap(lds[cc]), start=(cc == 0), stop=(cc == 7))
                return last
            ts_mm = p.step("tensor", "ts", t_mm)
            rs_x.set_free(lds, ts_mm)
            if ncc == 7:
                wg_free[wb] = ts_mm

            def v_rope(q, ncc=ncc, w=ts_mm, wrt=rt_free):
                q.wait_ge(p.h("ts"), w)
                if wrt:
                    q.wait_ge(p.h("ss"), wrt)
                sl = slice(ncc * 512, (ncc + 1) * 512)
                q.tensor_tensor(out=rtile[:], in0=pa[:], in1=cosb[:, sl], op=ALU.mult)
                q.tensor_tensor(out=xmul[:], in0=pb_[:], in1=sinb[:, sl], op=ALU.mult)
                return q.tensor_add(out=rtile[:], in0=rtile[:], in1=xmul[:])
            vs_rope = p.step("vector", "vs", v_rope)
            pa_free = vs_rope

            ob, oprev = oring.acquire()

            def s_out(q, dt=dt, w=vs_rope, ob=ob, oprev=oprev):
                q.wait_ge(p.h("vs"), w)
                if oprev is not None:
                    q.wait_ge(p.h(oprev[0]), oprev[1])
                return q.activation(out=oring.bufs[ob][:], in_=rtile[:], func=AF.Copy,
                                    bias=0.0, scale=(scl128[:, 0:1] if dt < 8 else 1.0))
            ss_out = p.step("scalar", "ss", s_out)
            rt_free = ss_out
            dst = QrT_o if dt < 8 else KrT_o
            r0 = (dt % 8) * 128
            oring.store(ob, dst[r0:r0 + 128, ncc * 512:(ncc + 1) * 512], ss_out)

    wstv2 = p.sb("wstv2", [128, 4096], BF16)
    vwl = []
    for cc in range(8):
        def f(q, cc=cc):
            return q.dma_start(out=wstv[:, cc * 512:(cc + 1) * 512],
                               in_=qkvT_i[cc * 128:(cc + 1) * 128, 2048:2560])
        vwl.append(p.dma("gpsimd", "lw", f))
    for cc in range(8):
        def f(q, cc=cc):
            return q.dma_start(out=wstv2[:, cc * 512:(cc + 1) * 512],
                               in_=qkvT_i[cc * 128:(cc + 1) * 128, 2560:3072])
        vwl.append(p.dma("gpsimd", "lw", f))

    pv_free = 0
    st_gate = 0
    for tt in range(32):
        stl = []
        for cc in range(8):
            def f(q, cc=cc, tt=tt, w=st_gate):
                if w and cc == 0:
                    q.wait_ge(p.h("ts"), w)
                return q.dma_start(out=xrst[cc][:],
                                   in_=xrT_s[cc * 128:(cc + 1) * 128,
                                             tt * 128:(tt + 1) * 128])
            stl.append(p.dma("sync", "lx", f))

        ts_v = None
        for half in range(2):
            grid = wstv if half == 0 else wstv2

            def t_v(q, tt=tt, half=half, stl=stl, grid=grid, wpv=pv_free, vwl=vwl):
                if wpv:
                    q.wait_ge(p.h("ss"), wpv)
                if tt == 0 and half == 0:
                    for s_ in vwl:
                        q.wait_ge(p.h(s_[0]), s_[1])
                for cc in range(8):
                    q.wait_ge(p.h(stl[cc][0]), stl[cc][1])
                    last = q.matmul(pb_[:], xrst[cc][:], grid[:, cc * 512:(cc + 1) * 512],
                                    start=(cc == 0), stop=(cc == 7))
                return last
            ts_v = p.step("tensor", "ts", t_v)

            ob, oprev = oring.acquire()

            def s_ev(q, w=ts_v, ob=ob, oprev=oprev):
                q.wait_ge(p.h("ts"), w)
                if oprev is not None:
                    q.wait_ge(p.h(oprev[0]), oprev[1])
                return q.activation(out=oring.bufs[ob][:], in_=pb_[:], func=AF.Copy,
                                    bias=0.0, scale=1.0)
            ss_ev = p.step("scalar", "ss", s_ev)
            pv_free = ss_ev
            oring.store(ob, V_o[tt * 128:(tt + 1) * 128, half * 512:(half + 1) * 512], ss_ev)
        st_gate = ts_v

    so_t = p.pool_totals("so")
    p.on("gpsimd", lambda q: p.emit_pool_wait(q, so_t))
    nc = p.build()
    return p, nc


def host_rope64():
    inv_freq = 1.0 / 10000.0 ** (np.arange(0, 64, 2, dtype=np.float32) / 64)
    t = np.arange(T, dtype=np.float32)
    fr = np.outer(inv_freq, t)
    c = np.tile(np.cos(fr).astype(np.float32), (4, 1))
    s = np.tile(np.sin(fr).astype(np.float32), (4, 1))
    return c, s





NH, HD = 16, 64


def build_F1b(core_id):
    p = Prog()
    QrT_i = p.din("QrT", [D, T], BF16)
    KrT_i = p.din("KrT", [D, T], BF16)
    V_i = p.din("V", [T, D], BF16)
    maskt_i = p.din("maskt", [4, 128, 512], F32)
    yT_o = p.dout("yT", [D, T], BF16)

    Qt = [p.sb(f"qt{i}", [64, T], BF16) for i in range(2)]
    Kt = [p.sb(f"kt{i}", [64, T], BF16) for i in range(2)]
    Vh = [p.sb(f"vh{i}", [128, 32 * HD], BF16) for i in range(2)]
    maskt = p.sb("masktb", [128, 4 * 512], F32)
    WTc = [p.sb(f"wtcb{i}", [128, 512], BF16) for i in range(4)]
    wsub = [p.sb(f"wsubb{i}", [128, 512], F32) for i in range(2)]
    rrs = p.sb("rrsb1", [1, 512], F32)
    rrsb = p.sb("rrsbb", [64, 512], F32)
    ones64 = p.sb("ones64b", [1, 64], F32)
    onescb = p.sb("onescbb", [128, 1], BF16)

    psT = [p.ps(f"psTb{i}", [128, 512], F32) for i in range(4)]
    pav = p.ps("pavb", [64, 512], F32)
    psm = p.ps("psmb", [1, 512], F32)
    pmb = p.ps("pmbb", [64, 512], F32)

    for nm in ["vs", "ss", "ts"]:
        p.sem(nm)
    p.dma_pool("lk", 4)
    p.dma_pool("lv", 4)
    p.dma_pool("so", 4)

    oev = [p.sb(f"oevb{i}", [64, 512], BF16) for i in range(3)]
    ostore_prev = {}

    cl = [p.dma("sync", "lv", lambda q, a=a: q.dma_start(
        out=maskt[:, a * 512:(a + 1) * 512], in_=maskt_i[a, :, :])) for a in range(4)]

    def vinit(q):
        for s_ in cl:
            q.wait_ge(p.h(s_[0]), s_[1])
        q.memset(ones64[:], 1.0)
        return q.memset(onescb[:], 1.0)
    vs_init = p.step("vector", "vs", vinit)

    head_end = [0, 0]
    psT_guard = [0, 0]
    wtc_guard = [0, 0]
    pav_guard = vs_init
    ev_i = 0

    for h in range(NH):
        b = h % 2
        lds = []
        for i in range(8):
            def fk(q, h=h, b=b, i=i, w=head_end[b]):
                if w and i == 0:
                    q.wait_ge(p.h("ts"), w)
                return q.dma_start(out=Kt[b][:, i * 512:(i + 1) * 512],
                                   in_=KrT_i[h * 64:(h + 1) * 64, i * 512:(i + 1) * 512])
            lds.append(p.dma("gpsimd", "lk", fk))
        for i in range(8):
            def fq_(q, h=h, b=b, i=i):
                return q.dma_start(out=Qt[b][:, i * 512:(i + 1) * 512],
                                   in_=QrT_i[h * 64:(h + 1) * 64, i * 512:(i + 1) * 512])
            lds.append(p.dma("gpsimd", "lk", fq_))
        for jt in range(32):
            def fv(q, h=h, b=b, jt=jt, w=head_end[b]):
                if w and jt == 0:
                    q.wait_ge(p.h("ts"), w)
                return q.dma_start(out=Vh[b][:, jt * HD:(jt + 1) * HD],
                                   in_=V_i[jt * 128:(jt + 1) * 128, h * 64:(h + 1) * 64])
            lds.append(p.dma("sync", "lv", fv))

        last_ts = 0
        for qc in range(8):
            njt = 4 * qc + 4
            for jp in range(njt // 2):
                pb = jp % 2
                i0, i1 = pb * 2, pb * 2 + 1
                jt0 = 2 * jp
                isdiag = jt0 >= 4 * qc

                def t_s(q, b=b, qc=qc, jt0=jt0, i0=i0, i1=i1, lds=lds, wfree=psT_guard[pb],
                        first=(qc == 0 and jp == 0)):
                    if first:
                        for s_ in lds:
                            q.wait_ge(p.h(s_[0]), s_[1])
                    if wfree:
                        q.wait_ge(p.h("ss"), wfree)
                    q.matmul(psT[i0][:], Kt[b][:, jt0 * 128:(jt0 + 1) * 128],
                             Qt[b][:, qc * 512:(qc + 1) * 512], start=True, stop=True)
                    return q.matmul(psT[i1][:], Kt[b][:, (jt0 + 1) * 128:(jt0 + 2) * 128],
                                    Qt[b][:, qc * 512:(qc + 1) * 512], start=True, stop=True)
                ts_s = p.step("tensor", "ts", t_s)

                if isdiag:
                    def v_sub(q, qc=qc, jt0=jt0, i0=i0, i1=i1, w=ts_s):
                        q.wait_ge(p.h("ts"), w)
                        v0 = jt0 - 4 * qc
                        q.tensor_tensor(out=wsub[0][:], in0=psT[i0][:],
                                        in1=maskt[:, v0 * 512:(v0 + 1) * 512], op=ALU.add)
                        return q.tensor_tensor(out=wsub[1][:], in0=psT[i1][:],
                                               in1=maskt[:, (v0 + 1) * 512:(v0 + 2) * 512],
                                               op=ALU.add)
                    vs_sub = p.step("vector", "vs", v_sub)

                    def s_exp(q, i0=i0, i1=i1, w=vs_sub, wav=wtc_guard[pb]):
                        q.wait_ge(p.h("vs"), w)
                        if wav:
                            q.wait_ge(p.h("ts"), wav)
                        q.activation(out=WTc[i0][:], in_=wsub[0][:], func=AF.Exp,
                                     bias=0.0, scale=1.0)
                        return q.activation(out=WTc[i1][:], in_=wsub[1][:], func=AF.Exp,
                                            bias=0.0, scale=1.0)
                    ss_exp = p.step("scalar", "ss", s_exp)
                else:
                    def s_exp(q, i0=i0, i1=i1, w=ts_s, wav=wtc_guard[pb]):
                        q.wait_ge(p.h("ts"), w)
                        if wav:
                            q.wait_ge(p.h("ts"), wav)
                        q.activation(out=WTc[i0][:], in_=psT[i0][:], func=AF.Exp,
                                     bias=0.0, scale=1.0)
                        return q.activation(out=WTc[i1][:], in_=psT[i1][:], func=AF.Exp,
                                            bias=0.0, scale=1.0)
                    ss_exp = p.step("scalar", "ss", s_exp)
                psT_guard[pb] = ss_exp

                def t_av(q, b=b, qc=qc, jp=jp, jt0=jt0, i0=i0, i1=i1, w=ss_exp,
                         first=(jp == 0), wpav=pav_guard):
                    q.wait_ge(p.h("ss"), w)
                    if first:
                        q.wait_ge(p.h("vs"), wpav)
                    laststop = (jt0 + 1 == 4 * qc + 3)
                    q.matmul(pav[:], Vh[b][:, jt0 * HD:(jt0 + 1) * HD], WTc[i0][:],
                             start=first, stop=False)
                    q.matmul(pav[:], Vh[b][:, (jt0 + 1) * HD:(jt0 + 2) * HD], WTc[i1][:],
                             start=False, stop=laststop)
                    q.matmul(psm[:], onescb[:], WTc[i0][:], start=first, stop=False)
                    return q.matmul(psm[:], onescb[:], WTc[i1][:], start=False, stop=laststop)
                ts_av = p.step("tensor", "ts", t_av)
                wtc_guard[pb] = ts_av
                last_ts = ts_av

            def v_f1(q, w=last_ts):
                q.wait_ge(p.h("ts"), w)
                return q.reciprocal(out=rrs[:], in_=psm[:])
            vs_f1 = p.step("vector", "vs", v_f1)

            def t_rb(q, w=vs_f1):
                q.wait_ge(p.h("vs"), w)
                return q.matmul(pmb[0:64, :], ones64[:], rrs[:], start=True, stop=True)
            ts_rb = p.step("tensor", "ts", t_rb)

            ob = ev_i % 3
            ev_i += 1

            def v_f2(q, ob=ob, w=ts_rb, wso=ostore_prev.get(ob)):
                q.wait_ge(p.h("ts"), w)
                if wso is not None:
                    q.wait_ge(p.h(wso[0]), wso[1])
                q.tensor_copy(out=rrsb[:], in_=pmb[0:64, :])
                return q.tensor_tensor(out=oev[ob][:], in0=pav[:], in1=rrsb[:], op=ALU.mult)
            vs_f2 = p.step("vector", "vs", v_f2)
            pav_guard = vs_f2

            def st_o(q, h=h, qc=qc, ob=ob, w=vs_f2):
                q.wait_ge(p.h("vs"), w)
                return q.dma_start(out=yT_o[h * 64:(h + 1) * 64, qc * 512:(qc + 1) * 512],
                                   in_=oev[ob][:])
            ostore_prev[ob] = p.dma("sync", "so", st_o)
        head_end[b] = last_ts

    so_t = p.pool_totals("so")
    p.on("gpsimd", lambda q: p.emit_pool_wait(q, so_t))
    nc = p.build()
    return p, nc





E1 = 8


def build_F1c(core_id):
    p = Prog()
    xnf_i = p.din("xn1Tf", [D, T], F32)
    yT_i = p.din("yT", [D, T], BF16)
    projT_i = p.din("projT", [D, D], F32)
    w2_i = p.din("w2c", [128, 8], F32)
    routT_i = p.din("routT", [D, E1], F32)
    ident_i = p.din("ident", [128, 128], F32)

    aT_o = p.dout("aT", [D, T], F32)
    xmT_o = p.dout("xmT", [D, T], BF16)
    gateT_o = p.dout("gateT", [E1, T], F32)

    wstp = p.sb("wstp", [128, 8192], BF16)
    routg = p.sb("routg", [128, 8 * E1], F32)
    routg2 = p.sb("routg2", [128, 8 * E1], F32)
    w2st = p.sb("w2st", [128, 8], F32)
    awide = p.sb("awide", [128, 4096], F32)
    xmwide = p.sb("xmwide", [128, 4096], BF16)
    sq = [p.sb(f"sqc{i}", [128, 512], F32) for i in range(2)]
    xnst = [p.sb(f"xnstc{i}", [128, 512], F32) for i in range(2)]
    rs2row = p.sb("rs2rowc", [1, T], F32)
    rs2b = p.sb("rs2bc", [128, 512], F32)
    xmfw = p.sb("xmfwc", [128, 4096], F32)
    consts = p.sb("constsc", [1, 4], F32)
    onesr = p.sb("onesrc", [1, 128], F32)
    onescf = p.sb("onescfc", [128, 1], F32)
    ident = p.sb("identc2", [128, 128], F32)
    grows = p.sb("growsc", [E1, T], F32)
    top8 = p.sb("top8c", [128, 8], F32)
    oh0 = p.sb("oh0c", [128, 8], F32)
    oh1 = p.sb("oh1c", [128, 8], F32)
    l2t = p.sb("l2tc", [128, 8], F32)
    gtile = p.sb("gtilec", [128, 8], F32)
    d01 = p.sb("d01c", [128, 1], F32)
    erow = p.sb("erowc", [128, 1], F32)
    recr = p.sb("recrc", [128, 1], F32)
    w1r = p.sb("w1rc", [128, 1], F32)

    pa = p.ps("pac2", [128, 512], F32)
    psq = p.ps("psqc", [1, 512], F32)
    pbc = p.ps("pbcc", [128, 512], F32)
    pl = p.ps("plc", [128, 8], F32)
    ptr8 = p.ps("ptr8c", [8, 128], F32)

    for nm in ["vs", "ss", "ts"]:
        p.sem(nm)
    p.dma_pool("lx", 4)
    p.dma_pool("lw", 4)
    p.dma_pool("so", 6)

    rs_y = RhsStream(p, "rsy", [128, 512], BF16, n=10, pool="lx", gate_sem="ts")
    orf = OutRing(p, "orfc", [128, 512], F32, n=3, pool="so", prod_sem="vs")
    orb = OutRing(p, "orbc", [128, 512], BF16, n=3, pool="so", prod_sem="ss")

    cl = []
    for cc in range(8):
        for oc in range(8):
            def f(q, cc=cc, oc=oc):
                return q.dma_start(out=wstp[:, oc * 1024 + cc * 128:oc * 1024 + (cc + 1) * 128],
                                   in_=projT_i[cc * 128:(cc + 1) * 128, oc * 128:(oc + 1) * 128])
            cl.append(p.dma("gpsimd", "lw", f))
    for cc in range(8):
        def f(q, cc=cc):
            return q.dma_start(out=routg[:, cc * E1:(cc + 1) * E1],
                               in_=routT_i[cc * 128:(cc + 1) * 128, :])
        cl.append(p.dma("sync", "lw", f))
    cl.append(p.dma("sync", "lw", lambda q: q.dma_start(out=w2st[:], in_=w2_i[:, :])))
    cl.append(p.dma("sync", "lw", lambda q: q.dma_start(out=ident[:], in_=ident_i[:, :])))

    def v_rg(q):
        for s_ in cl:
            q.wait_ge(p.h(s_[0]), s_[1])
        for cc in range(8):
            last = q.tensor_scalar(out=routg2[:, cc * E1:(cc + 1) * E1],
                                   in0=routg[:, cc * E1:(cc + 1) * E1],
                                   scalar1=w2st[:, cc:cc + 1], scalar2=None, op0=ALU.mult)
        return last
    p.step("vector", "vs", v_rg)

    def vinit(q):
        q.memset(onesr[:], 1.0)
        q.memset(onescf[:], 1.0)
        q.memset(consts[:, 0:1], 1e-6)
        q.memset(consts[:, 1:2], -0.5)
        for s_ in cl:
            q.wait_ge(p.h(s_[0]), s_[1])
        return q.memset(consts[:, 2:3], 0.0)
    vs_init = p.step("vector", "vs", vinit)

    aw_free = [0] * 8
    xmf_gate = [0] * 8
    ast_prev = [None] * 8
    xst_prev = [None] * 8
    xm_free = 0
    sq_free = [0, 0]
    row_read = 0
    bc_read = vs_init
    pl_free = 0
    g_end = vs_init

    for ncc in range(8):
        ts_sq = None
        for oc in range(8):
            lds = [rs_y.load(yT_i[cc * 128:(cc + 1) * 128, ncc * 512:(ncc + 1) * 512])
                   for cc in range(8)]
            lxn = p.dma("sync", "lx", lambda q, oc=oc, ncc=ncc, b=oc % 2, w=aw_free[oc]:
                        ((q.wait_ge(p.h("vs"), w) if w else None),
                         q.dma_start(out=xnst[oc % 2][:],
                                     in_=xnf_i[oc * 128:(oc + 1) * 128,
                                               ncc * 512:(ncc + 1) * 512]))[-1])

            def t_mm(q, lds=lds, oc=oc, ncc=ncc, first=(ncc == 0 and oc == 0), vi=vs_init):
                if first:
                    q.wait_ge(p.h("vs"), vi)
                for cc in range(8):
                    rs_y.wait(q, lds[cc])
                    last = q.matmul(pa[:], wstp[:, oc * 1024 + cc * 128:oc * 1024 + (cc + 1) * 128],
                                    rs_y.ap(lds[cc]), start=(cc == 0), stop=(cc == 7))
                return last
            ts_mm = p.step("tensor", "ts", t_mm)
            rs_y.set_free(lds, ts_mm)

            def v_ev(q, oc=oc, lxn=lxn, w=ts_mm, wfree=aw_free[oc], wst=ast_prev[oc]):
                q.wait_ge(p.h("ts"), w)
                q.wait_ge(p.h(lxn[0]), lxn[1])
                if wfree:
                    q.wait_ge(p.h("vs"), wfree)
                if wst is not None:
                    q.wait_ge(p.h(wst[0]), wst[1])
                return q.tensor_tensor(out=awide[:, oc * 512:(oc + 1) * 512],
                                       in0=pa[:], in1=xnst[oc % 2][:], op=ALU.add)
            vs_ev = p.step("vector", "vs", v_ev)

            def _noop():
                pass
            def st_a(q, oc=oc, ncc=ncc, w=vs_ev):
                q.wait_ge(p.h("vs"), w)
                return q.dma_start(out=aT_o[oc * 128:(oc + 1) * 128, ncc * 512:(ncc + 1) * 512],
                                   in_=awide[:, oc * 512:(oc + 1) * 512])
            ast_prev[oc] = p.dma("sync", "so", st_a)

            b = oc % 2

            def s_sq(q, oc=oc, b=b, w=vs_ev, wfree=sq_free[b]):
                q.wait_ge(p.h("vs"), w)
                if wfree:
                    q.wait_ge(p.h("ts"), wfree)
                return q.activation(out=sq[b][:], in_=awide[:, oc * 512:(oc + 1) * 512],
                                    func=AF.Square, bias=0.0, scale=1.0)
            ss_sq = p.step("scalar", "ss", s_sq)

            def t_sq(q, oc=oc, b=b, w=ss_sq, wrr=row_read, wev=vs_ev):
                q.wait_ge(p.h("ss"), w)
                if oc == 0 and wrr:
                    q.wait_ge(p.h("vs"), wrr)
                if oc < 7:
                    q.wait_ge(p.h("vs"), wev)
                return q.matmul(psq[:], onescf[:], sq[b][:], start=(oc == 0), stop=(oc == 7))
            ts_sq = p.step("tensor", "ts", t_sq)
            sq_free[b] = ts_sq

        def v_row(q, ncc=ncc, w=ts_sq):
            q.wait_ge(p.h("ts"), w)
            return q.tensor_scalar(out=rs2row[:, ncc * 512:(ncc + 1) * 512], in0=psq[:],
                                   scalar1=1.0 / 1024.0, scalar2=None, op0=ALU.mult)
        vs_row = p.step("vector", "vs", v_row)
        row_read = vs_row

        def s_rstd(q, ncc=ncc, w=vs_row):
            q.wait_ge(p.h("vs"), w)
            sl = slice(ncc * 512, (ncc + 1) * 512)
            q.activation(out=rs2row[:, sl], in_=rs2row[:, sl], func=AF.Ln,
                         bias=consts[0:1, 0:1], scale=1.0)
            return q.activation(out=rs2row[:, sl], in_=rs2row[:, sl], func=AF.Exp,
                                bias=0.0, scale=consts[0:1, 1:2])
        ss_rstd = p.step("scalar", "ss", s_rstd)

        def t_bc(q, ncc=ncc, w=ss_rstd, wbc=bc_read):
            q.wait_ge(p.h("ss"), w)
            q.wait_ge(p.h("vs"), wbc)
            return q.matmul(pbc[:], onesr[:], rs2row[:, ncc * 512:(ncc + 1) * 512],
                            start=True, stop=True)
        ts_bc = p.step("tensor", "ts", t_bc)

        def v_bc(q, w=ts_bc):
            q.wait_ge(p.h("ts"), w)
            return q.tensor_copy(out=rs2b[:], in_=pbc[:])
        vs_bc = p.step("vector", "vs", v_bc)
        bc_read = vs_bc

        vs_xm_last = 0
        for oc in range(8):
            def v_xm(q, oc=oc, w=vs_bc, wxf=xm_free, wsc=xmf_gate[oc]):
                q.wait_ge(p.h("vs"), w)
                if oc == 0 and wxf:
                    q.wait_ge(p.h("ts"), wxf)
                if wsc:
                    q.wait_ge(p.h("ss"), wsc)
                return q.tensor_tensor(out=xmfw[:, oc * 512:(oc + 1) * 512],
                                       in0=awide[:, oc * 512:(oc + 1) * 512],
                                       in1=rs2b[:], op=ALU.mult)
            vs_xm = p.step("vector", "vs", v_xm)
            aw_free[oc] = vs_xm
            vs_xm_last = vs_xm

            def s_xm(q, oc=oc, w=vs_xm, wst=xst_prev[oc]):
                q.wait_ge(p.h("vs"), w)
                if wst is not None:
                    q.wait_ge(p.h(wst[0]), wst[1])
                return q.activation(out=xmwide[:, oc * 512:(oc + 1) * 512],
                                    in_=xmfw[:, oc * 512:(oc + 1) * 512],
                                    func=AF.Copy, bias=0.0, scale=w2st[:, oc:oc + 1])
            ss_xm = p.step("scalar", "ss", s_xm)
            xmf_gate[oc] = ss_xm

            def st_x(q, oc=oc, ncc=ncc, w=ss_xm):
                q.wait_ge(p.h("ss"), w)
                return q.dma_start(out=xmT_o[oc * 128:(oc + 1) * 128, ncc * 512:(ncc + 1) * 512],
                                   in_=xmwide[:, oc * 512:(oc + 1) * 512])
            xst_prev[oc] = p.dma("sync", "so", st_x)

        for tl in range(4):
            def t_pl(q, tl=tl, w=vs_xm_last, wpl=pl_free):
                q.wait_ge(p.h("vs"), w)
                if wpl:
                    q.wait_ge(p.h("vs"), wpl)
                for cc in range(8):
                    last = q.matmul(pl[:], xmfw[:, cc * 512 + tl * 128:cc * 512 + (tl + 1) * 128],
                                    routg2[:, cc * E1:(cc + 1) * E1], start=(cc == 0), stop=(cc == 7))
                return last
            ts_pl = p.step("tensor", "ts", t_pl)
            xm_free = ts_pl

            def v_g1(q, w=ts_pl):
                q.wait_ge(p.h("ts"), w)
                q.max(out=top8[:], in_=pl[:])
                q.tensor_scalar(out=oh0[:], in0=pl[:], scalar1=top8[:, 0:1], scalar2=None,
                                op0=ALU.is_equal)
                q.scalar_tensor_tensor(out=l2t[:], in0=oh0[:], scalar=-65536.0, in1=pl[:],
                                       op0=ALU.mult, op1=ALU.add)
                q.tensor_scalar(out=oh1[:], in0=l2t[:], scalar1=top8[:, 1:2], scalar2=None,
                                op0=ALU.is_equal)
                return q.tensor_sub(out=d01[:], in0=top8[:, 1:2], in1=top8[:, 0:1])
            vs_g1 = p.step("vector", "vs", v_g1)
            pl_free = vs_g1

            def s_e(q, w=vs_g1):
                q.wait_ge(p.h("vs"), w)
                return q.activation(out=erow[:], in_=d01[:], func=AF.Exp, bias=0.0, scale=1.0)
            ss_e = p.step("scalar", "ss", s_e)

            def v_g2(q, w=ss_e):
                q.wait_ge(p.h("ss"), w)
                q.tensor_scalar(out=recr[:], in0=erow[:], scalar1=1.0, scalar2=None, op0=ALU.add)
                q.reciprocal(out=recr[:], in_=recr[:])
                q.tensor_tensor(out=w1r[:], in0=erow[:], in1=recr[:], op=ALU.mult)
                q.tensor_scalar(out=gtile[:], in0=oh0[:], scalar1=recr[:], scalar2=None,
                                op0=ALU.mult)
                return q.scalar_tensor_tensor(out=gtile[:], in0=oh1[:], scalar=w1r[:],
                                              in1=gtile[:], op0=ALU.mult, op1=ALU.add)
            vs_g2 = p.step("vector", "vs", v_g2)

            def t_tr(q, w=vs_g2, wg=g_end):
                q.wait_ge(p.h("vs"), w)
                q.wait_ge(p.h("vs"), wg)
                return q.transpose(ptr8[:], gtile[:], ident[:])
            ts_tr = p.step("tensor", "ts", t_tr)

            def v_g3(q, ncc=ncc, tl=tl, w=ts_tr):
                q.wait_ge(p.h("ts"), w)
                return q.tensor_copy(out=grows[:, (ncc * 4 + tl) * 128:(ncc * 4 + tl + 1) * 128],
                                     in_=ptr8[:])
            g_end = p.step("vector", "vs", v_g3)

    def st_g(q, w=g_end):
        q.wait_ge(p.h("vs"), w)
        return q.dma_start(out=gateT_o[:, :], in_=grows[:, :])
    p.dma("sync", "so", st_g)

    so_t = p.pool_totals("so")
    p.on("gpsimd", lambda q: p.emit_pool_wait(q, so_t))
    nc = p.build()
    return p, nc





HFF = 4096
E1 = 8


def build_F2b(core_id, e_lo, e_hi, tag):
    p = Prog()
    xmT_i = p.din("xmT", [D, T], BF16)
    gateT_i = p.din("gateT", [E1, T], F32)
    init_i = p.din("macc_in", [D, T], F32)
    wgT_i = p.din("wgT", [E1, D, HFF], F32)
    wuT_i = p.din("wuT", [E1, D, HFF], F32)
    wdT_i = p.din("wdT", [E1, HFF, D], F32)
    out_o = p.dout("macc_out", [D, T], F32)

    xmq = p.sb("xmq2", [128, 8192], BF16)
    hq = p.sb("hq2", [128, 32768], BF16)
    wg_g = [p.sb(f"wgg{i}", [128, 1024], BF16) for i in range(2)]
    wu_g = [p.sb(f"wug{i}", [128, 1024], BF16) for i in range(2)]
    wd_g = [p.sb(f"wdg{i}", [128, 4096], BF16) for i in range(2)]
    sg = p.sb("sg2", [128, 512], F32)
    macc = p.sb("macc2", [128, 8192], F32)
    gateb = p.sb("gateb2", [128, 1024], F32)
    g1row = p.sb("g1row2", [1, 1024], F32)
    tmp = [p.sb(f"tmp2{i}", [128, 512], F32) for i in range(2)]
    onesr = p.sb("onesr2", [1, 128], F32)

    pg = p.ps("pg2", [128, 512], F32)
    pu = p.ps("pu2", [128, 512], F32)
    py = [p.ps(f"py2{i}", [128, 512], F32) for i in range(2)]
    pbc = p.ps("pbc2", [128, 512], F32)

    for nm in ["vs", "ss", "ts"]:
        p.sem(nm)
    p.dma_pool("lw", 6)
    p.dma_pool("lx", 4)
    p.dma_pool("so", 4)

    def vinit(q):
        return q.memset(onesr[:], 1.0)
    vs_init = p.step("vector", "vs", vinit)

    wgu_free = [0, 0]
    wd_free = [0, 0]
    hq_free = 0
    xmq_free = 0
    macc_free = [0] * 8
    mst_prev = [None] * 8
    pg_free = 0
    pu_free = 0
    py_free = [0, 0]
    gb_read = vs_init
    gb_load_free = 0

    for tq in range(4):
        xml = []
        for cc in range(8):
            def f(q, cc=cc, tq=tq, w=xmq_free):
                if w and cc == 0:
                    q.wait_ge(p.h("ts"), w)
                return q.dma_start(out=xmq[:, cc * 1024:(cc + 1) * 1024],
                                   in_=xmT_i[cc * 128:(cc + 1) * 128,
                                             tq * 1024:(tq + 1) * 1024])
            xml.append(p.dma("sync", "lx", f))

        iml = []
        for ct in range(8):
            def f(q, ct=ct, tq=tq, w=macc_free[ct], wst=mst_prev[ct]):
                if w:
                    q.wait_ge(p.h("vs"), w)
                if wst is not None:
                    q.wait_ge(p.h(wst[0]), wst[1])
                return q.dma_start(out=macc[:, ct * 1024:(ct + 1) * 1024],
                                   in_=init_i[ct * 128:(ct + 1) * 128,
                                              tq * 1024:(tq + 1) * 1024])
            iml.append(p.dma("sync", "lx", f))

        for e in range(e_lo, e_hi):
            b_e = e % 2
            lg = p.dma("sync", "lx", lambda q, e=e, tq=tq, w=gb_load_free:
                       ((q.wait_ge(p.h("ts"), w) if w else None),
                        q.dma_start(out=g1row[:],
                                    in_=gateT_i[e:e + 1, tq * 1024:(tq + 1) * 1024]))[-1])

            def t_gb1(q, lg=lg, wgb=gb_read):
                q.wait_ge(p.h("vs"), wgb)
                q.wait_ge(p.h(lg[0]), lg[1])
                return q.matmul(pbc[:], onesr[:], g1row[0:1, 0:512],
                                start=True, stop=True)
            ts_gb1 = p.step("tensor", "ts", t_gb1)

            def v_gb1(q, w=ts_gb1):
                q.wait_ge(p.h("ts"), w)
                return q.tensor_copy(out=gateb[:, 0:512], in_=pbc[:])
            vs_gb1 = p.step("vector", "vs", v_gb1)

            def t_gb2(q, w=vs_gb1):
                q.wait_ge(p.h("vs"), w)
                return q.matmul(pbc[:], onesr[:], g1row[0:1, 512:1024],
                                start=True, stop=True)
            ts_gb2 = p.step("tensor", "ts", t_gb2)
            gb_load_free = ts_gb2

            def v_gb2(q, w=ts_gb2):
                q.wait_ge(p.h("ts"), w)
                return q.tensor_copy(out=gateb[:, 512:1024], in_=pbc[:])
            gb_read = p.step("vector", "vs", v_gb2)

            vh_last = 0
            for ht in range(32):
                wb = ht % 2
                wl = []
                for cc in range(8):
                    def f(q, e=e, cc=cc, ht=ht, wb=wb, w=wgu_free[wb]):
                        if w and cc == 0:
                            q.wait_ge(p.h("ts"), w)
                        return q.dma_start(out=wg_g[wb][:, cc * 128:(cc + 1) * 128],
                                           in_=wgT_i[e, cc * 128:(cc + 1) * 128,
                                                     ht * 128:(ht + 1) * 128])
                    wl.append(p.dma("gpsimd", "lw", f))
                for cc in range(8):
                    def f(q, e=e, cc=cc, ht=ht, wb=wb):
                        return q.dma_start(out=wu_g[wb][:, cc * 128:(cc + 1) * 128],
                                           in_=wuT_i[e, cc * 128:(cc + 1) * 128,
                                                     ht * 128:(ht + 1) * 128])
                    wl.append(p.dma("gpsimd", "lw", f))

                ts_gu = None
                for nc2 in range(2):
                    def t_gu(q, ht=ht, wb=wb, nc2=nc2, wl=wl, xml=xml, wpg=pg_free, wpu=pu_free,
                             first=(ht == 0 and nc2 == 0), whq=hq_free):
                        if nc2 == 0:
                            for s_ in wl:
                                q.wait_ge(p.h(s_[0]), s_[1])
                        if first:
                            for s_ in xml:
                                q.wait_ge(p.h(s_[0]), s_[1])
                            if whq:
                                q.wait_ge(p.h("ts"), whq)
                        if wpg:
                            q.wait_ge(p.h("ss"), wpg)
                        if wpu:
                            q.wait_ge(p.h("vs"), wpu)
                        for cc in range(8):
                            q.matmul(pg[:], wg_g[wb][:, cc * 128:(cc + 1) * 128],
                                     xmq[:, cc * 1024 + nc2 * 512:cc * 1024 + (nc2 + 1) * 512],
                                     start=(cc == 0), stop=(cc == 7))
                        for cc in range(8):
                            last = q.matmul(pu[:], wu_g[wb][:, cc * 128:(cc + 1) * 128],
                                            xmq[:, cc * 1024 + nc2 * 512:cc * 1024 + (nc2 + 1) * 512],
                                            start=(cc == 0), stop=(cc == 7))
                        return last
                    ts_gu = p.step("tensor", "ts", t_gu)

                    def s_silu(q, w=ts_gu):
                        q.wait_ge(p.h("ts"), w)
                        return q.activation(out=sg[:], in_=pg[:], func=AF.Silu, bias=0.0, scale=1.0)
                    ss_silu = p.step("scalar", "ss", s_silu)
                    pg_free = ss_silu

                    def v_h(q, ht=ht, nc2=nc2, w=ss_silu, whq=hq_free):
                        q.wait_ge(p.h("ss"), w)
                        if whq and ht == 0 and nc2 == 0:
                            q.wait_ge(p.h("ts"), whq)
                        return q.tensor_tensor(out=hq[:, ht * 1024 + nc2 * 512:ht * 1024 + (nc2 + 1) * 512],
                                               in0=sg[:], in1=pu[:], op=ALU.mult)
                    vh_last = p.step("vector", "vs", v_h)
                    pu_free = vh_last
                wgu_free[wb] = ts_gu
            xmq_free = ts_gu

            for ct in range(8):
                wb = ct % 2
                wl = []
                for ht in range(32):
                    def f(q, e=e, ct=ct, ht=ht, wb=wb, w=wd_free[wb]):
                        if w and ht == 0:
                            q.wait_ge(p.h("ts"), w)
                        return q.dma_start(out=wd_g[wb][:, ht * 128:(ht + 1) * 128],
                                           in_=wdT_i[e, ht * 128:(ht + 1) * 128,
                                                     ct * 128:(ct + 1) * 128])
                    wl.append(p.dma("gpsimd", "lw", f))

                ts_d = None
                for nc2 in range(2):
                    def t_d(q, ct=ct, wb=wb, nc2=nc2, wl=wl, wpy=py_free[nc2], whq=vh_last):
                        if nc2 == 0:
                            for s_ in wl:
                                q.wait_ge(p.h(s_[0]), s_[1])
                            q.wait_ge(p.h("vs"), whq)
                        if wpy:
                            q.wait_ge(p.h("vs"), wpy)
                        for ht in range(32):
                            last = q.matmul(py[nc2][:], wd_g[wb][:, ht * 128:(ht + 1) * 128],
                                            hq[:, ht * 1024 + nc2 * 512:ht * 1024 + (nc2 + 1) * 512],
                                            start=(ht == 0), stop=(ht == 31))
                        return last
                    ts_d = p.step("tensor", "ts", t_d)

                    def v_acc(q, ct=ct, nc2=nc2, w=ts_d, iml=iml, first=(e == e_lo)):
                        q.wait_ge(p.h("ts"), w)
                        if first:
                            q.wait_ge(p.h(iml[ct][0]), iml[ct][1])
                        q.tensor_tensor(out=tmp[nc2][:], in0=py[nc2][:],
                                        in1=gateb[:, nc2 * 512:(nc2 + 1) * 512], op=ALU.mult)
                        sl = slice(ct * 1024 + nc2 * 512, ct * 1024 + (nc2 + 1) * 512)
                        return q.tensor_tensor(out=macc[:, sl], in0=macc[:, sl],
                                               in1=tmp[nc2][:], op=ALU.add)
                    vs_acc = p.step("vector", "vs", v_acc)
                    py_free[nc2] = vs_acc
                    if e == e_hi - 1:
                        macc_free[ct] = vs_acc
                wd_free[wb] = ts_d
            hq_free = ts_d

        for ct in range(8):
            def st_m(q, ct=ct, tq=tq, w=macc_free[ct]):
                q.wait_ge(p.h("vs"), w)
                return q.dma_start(out=out_o[ct * 128:(ct + 1) * 128, tq * 1024:(tq + 1) * 1024],
                                   in_=macc[:, ct * 1024:(ct + 1) * 1024])
            mst_prev[ct] = p.dma("sync", "so", st_m)

    so_t = p.pool_totals("so")
    p.on("gpsimd", lambda q: p.emit_pool_wait(q, so_t))
    nc = p.build()
    return p, nc






def build_F3a(core_id):
    p = Prog()
    xnb_i = p.din("xn1Tb", [D, T], BF16)
    WqT_i = p.din("WqT", [D, D], F32)
    WkT_i = p.din("WkT", [D, D], F32)
    WvT_i = p.din("WvT", [D, D], F32)
    bq_i = p.din("bqg", [128, 8], F32)
    bk_i = p.din("bkg", [128, 8], F32)
    bv_i = p.din("bvr", [1, D], F32)
    temp_i = p.din("temp", [1, 1], F32)

    QT_o = p.dout("QT3", [D, T], BF16)
    KT_o = p.dout("KT3", [D, T], BF16)
    V_o = p.dout("V3", [T, D], BF16)

    wst = [p.sb(f"wst3{i}", [128, 1024], BF16) for i in range(2)]
    wstv = p.sb("wstv3", [128, 4096], BF16)
    wstv2 = p.sb("wstv3b", [128, 4096], BF16)
    bqg = p.sb("bqg3", [128, 8], F32)
    bkg = p.sb("bkg3", [128, 8], F32)
    bqs = p.sb("bqs3", [128, 8], F32)
    bvr = p.sb("bvr3", [1, D], F32)
    bvb = p.sb("bvb3", [128, 1024], F32)
    tmp1 = p.sb("tmp13", [1, 1], F32)
    sq128 = p.sb("sq1283", [128, 1], F32)
    onesr = p.sb("onesr3", [1, 128], F32)
    xnst = [p.sb(f"xnst3{i}", [128, 128], BF16) for i in range(8)]
    vadd = p.sb("vadd3", [128, 512], F32)

    pa = p.ps("pa3", [128, 512], F32)
    pb_ = p.ps("pb3", [128, 512], F32)
    pt1 = p.ps("pt13", [128, 1], F32)

    for nm in ["vs", "ss", "ts"]:
        p.sem(nm)
    p.dma_pool("lx", 4)
    p.dma_pool("lw", 4)
    p.dma_pool("so", 6)

    rs_x = RhsStream(p, "rsx3", [128, 512], BF16, n=10, pool="lx", gate_sem="ts")
    oring = OutRing(p, "orng3", [128, 512], BF16, n=4, pool="so", prod_sem="vs")

    cl = []
    cl.append(p.dma("sync", "lw", lambda q: q.dma_start(out=bqg[:], in_=bq_i[:, :])))
    cl.append(p.dma("sync", "lw", lambda q: q.dma_start(out=bkg[:], in_=bk_i[:, :])))
    cl.append(p.dma("sync", "lw", lambda q: q.dma_start(out=bvr[:], in_=bv_i[:, :])))
    cl.append(p.dma("sync", "lw", lambda q: q.dma_start(out=tmp1[:], in_=temp_i[:, :])))

    def vinit(q):
        return q.memset(onesr[:], 1.0)
    vs_init = p.step("vector", "vs", vinit)

    def s_t1(q, cl=cl):
        for s_ in cl:
            q.wait_ge(p.h(s_[0]), s_[1])
        return q.activation(out=tmp1[:], in_=tmp1[:], func=AF.Abs, bias=0.0, scale=1.0)
    ss_t1 = p.step("scalar", "ss", s_t1)

    def v_t2(q, w=ss_t1):
        q.wait_ge(p.h("ss"), w)
        q.tensor_scalar(out=tmp1[:], in0=tmp1[:], scalar1=0.1, scalar2=8.0,
                        op0=ALU.add, op1=ALU.mult)
        return q.reciprocal(out=tmp1[:], in_=tmp1[:])
    vs_t2 = p.step("vector", "vs", v_t2)

    def t_bc1(q, w=vs_t2, wi=vs_init):
        q.wait_ge(p.h("vs"), w)
        return q.matmul(pt1[:], onesr[:], tmp1[:], start=True, stop=True)
    ts_bc1 = p.step("tensor", "ts", t_bc1)

    def v_bc1(q, w=ts_bc1):
        q.wait_ge(p.h("ts"), w)
        q.tensor_copy(out=sq128[:], in_=pt1[:])
        return q.tensor_scalar(out=bqs[:], in0=bqg[:], scalar1=sq128[:, 0:1], scalar2=None,
                               op0=ALU.mult)
    vs_bc1 = p.step("vector", "vs", v_bc1)

    def t_bv(q, w=vs_bc1):
        q.wait_ge(p.h("vs"), w)
        return q.matmul(pa[:], onesr[:], bvr[0:1, 0:512], start=True, stop=True)
    ts_bv = p.step("tensor", "ts", t_bv)

    def v_bv(q, w=ts_bv):
        q.wait_ge(p.h("ts"), w)
        return q.tensor_copy(out=bvb[:, 0:512], in_=pa[:])
    vs_bv = p.step("vector", "vs", v_bv)

    def t_bv2(q, w=vs_bv):
        q.wait_ge(p.h("vs"), w)
        return q.matmul(pa[:], onesr[:], bvr[0:1, 512:1024], start=True, stop=True)
    ts_bv2 = p.step("tensor", "ts", t_bv2)

    def v_bv2(q, w=ts_bv2):
        q.wait_ge(p.h("ts"), w)
        return q.tensor_copy(out=bvb[:, 512:1024], in_=pa[:])
    vs_bv2 = p.step("vector", "vs", v_bv2)

    wg_free = [0, 0]
    pa_free = ("vs", vs_bv2)
    ev_free = 0
    for dt in range(16):
        isq = dt < 8
        W_i = WqT_i if isq else WkT_i
        wb = dt % 2
        wl = []
        for cc in range(8):
            def f(q, cc=cc, dt=dt, wb=wb, W_i=W_i, wfree=wg_free[wb]):
                if wfree and cc == 0:
                    q.wait_ge(p.h("ts"), wfree)
                return q.dma_start(out=wst[wb][:, cc * 128:(cc + 1) * 128],
                                   in_=W_i[cc * 128:(cc + 1) * 128,
                                           (dt % 8) * 128:(dt % 8 + 1) * 128])
            wl.append(p.dma("gpsimd", "lw", f))

        for ncc in range(8):
            lds = [rs_x.load(xnb_i[cc * 128:(cc + 1) * 128, ncc * 512:(ncc + 1) * 512])
                   for cc in range(8)]

            def t_mm(q, lds=lds, wl=wl, wb=wb, ncc=ncc, wpa=pa_free):
                if wpa[1]:
                    q.wait_ge(p.h(wpa[0]), wpa[1])
                if ncc == 0:
                    for s_ in wl:
                        q.wait_ge(p.h(s_[0]), s_[1])
                for cc in range(8):
                    rs_x.wait(q, lds[cc])
                    last = q.matmul(pa[:], wst[wb][:, cc * 128:(cc + 1) * 128],
                                    rs_x.ap(lds[cc]), start=(cc == 0), stop=(cc == 7))
                return last
            ts_mm = p.step("tensor", "ts", t_mm)
            rs_x.set_free(lds, ts_mm)
            if ncc == 7:
                wg_free[wb] = ts_mm

            ob, oprev = oring.acquire()

            def v_out(q, dt=dt, isq=isq, w=ts_mm, ob=ob, oprev=oprev):
                q.wait_ge(p.h("ts"), w)
                if oprev is not None:
                    q.wait_ge(p.h(oprev[0]), oprev[1])
                if isq:
                    return q.tensor_scalar(out=oring.bufs[ob][:], in0=pa[:],
                                           scalar1=sq128[:, 0:1], scalar2=bqs[:, dt:dt + 1],
                                           op0=ALU.mult, op1=ALU.add)
                return q.tensor_scalar(out=oring.bufs[ob][:], in0=pa[:],
                                       scalar1=bkg[:, dt - 8:dt - 7], scalar2=None, op0=ALU.add)
            vs_out = p.step("vector", "vs", v_out)
            pa_free = ("vs", vs_out)
            dst = QT_o if isq else KT_o
            r0 = (dt % 8) * 128
            oring.store(ob, dst[r0:r0 + 128, ncc * 512:(ncc + 1) * 512], vs_out)

    vwl = []
    for cc in range(8):
        def f(q, cc=cc):
            return q.dma_start(out=wstv[:, cc * 512:(cc + 1) * 512],
                               in_=WvT_i[cc * 128:(cc + 1) * 128, 0:512])
        vwl.append(p.dma("gpsimd", "lw", f))
    for cc in range(8):
        def f(q, cc=cc):
            return q.dma_start(out=wstv2[:, cc * 512:(cc + 1) * 512],
                               in_=WvT_i[cc * 128:(cc + 1) * 128, 512:1024])
        vwl.append(p.dma("gpsimd", "lw", f))

    pv_free = 0
    st_gate = 0
    for tt in range(32):
        stl = []
        for cc in range(8):
            def f(q, cc=cc, tt=tt, w=st_gate):
                if w and cc == 0:
                    q.wait_ge(p.h("ts"), w)
                return q.dma_start(out=xnst[cc][:],
                                   in_=xnb_i[cc * 128:(cc + 1) * 128,
                                             tt * 128:(tt + 1) * 128])
            stl.append(p.dma("sync", "lx", f))

        ts_v = None
        for half in range(2):
            grid = wstv if half == 0 else wstv2

            def t_v(q, tt=tt, half=half, stl=stl, grid=grid, wpv=pv_free, vwl=vwl):
                if wpv:
                    q.wait_ge(p.h("vs"), wpv)
                if tt == 0 and half == 0:
                    for s_ in vwl:
                        q.wait_ge(p.h(s_[0]), s_[1])
                for cc in range(8):
                    q.wait_ge(p.h(stl[cc][0]), stl[cc][1])
                    last = q.matmul(pb_[:], xnst[cc][:], grid[:, cc * 512:(cc + 1) * 512],
                                    start=(cc == 0), stop=(cc == 7))
                return last
            ts_v = p.step("tensor", "ts", t_v)

            ob, oprev = oring.acquire()

            def v_ev(q, half=half, w=ts_v, ob=ob, oprev=oprev):
                q.wait_ge(p.h("ts"), w)
                if oprev is not None:
                    q.wait_ge(p.h(oprev[0]), oprev[1])
                return q.tensor_tensor(out=oring.bufs[ob][:], in0=pb_[:],
                                       in1=bvb[:, half * 512:(half + 1) * 512], op=ALU.add)
            vs_ev = p.step("vector", "vs", v_ev)
            pv_free = vs_ev
            oring.store(ob, V_o[tt * 128:(tt + 1) * 128, half * 512:(half + 1) * 512], vs_ev)
        st_gate = ts_v

    so_t = p.pool_totals("so")
    p.on("gpsimd", lambda q: p.emit_pool_wait(q, so_t))
    nc = p.build()
    return p, nc





NH, HD = 16, 64
FREQS = [31.0, 37.0, 41.0, 43.0, 47.0]
MAGIC = 1.5 * 2.0 ** 23
TWO_PI = 2.0 * float(np.pi)


def build_F3b(core_id):
    p = Prog()
    QT_i = p.din("QT3", [D, T], BF16)
    KT_i = p.din("KT3", [D, T], BF16)
    V_i = p.din("V3", [T, D], BF16)
    phi_i = p.din("phir", [1, 80], F32)
    alpha_i = p.din("alphar", [1, 80], F32)
    ident_i = p.din("ident", [128, 128], F32)
    yT_o = p.dout("yT3", [D, T], BF16)

    Qt = [p.sb(f"q3t{i}", [64, T], BF16) for i in range(2)]
    Kt = [p.sb(f"k3t{i}", [64, T], BF16) for i in range(2)]
    Vh = [p.sb(f"v3h{i}", [128, 32 * HD], BF16) for i in range(2)]
    uw = p.sb("uw3", [128, 2048], F32)
    sf = [p.sb(f"sf3{i}", [128, 2048], F32) for i in range(5)]
    s2t = [p.sb(f"s2t3{i}", [128, 2048], F32) for i in range(5)]
    pt = p.sb("pt3", [128, 2048], F32)
    mfb = [p.sb(f"mfb3{i}", [128, 2048], BF16) for i in range(3)]
    diag = p.sb("diag3", [128, 128], BF16)
    WT3 = [p.sb(f"wt3w{i}", [128, 2048], BF16) for i in range(2)]
    phirow = p.sb("phirow3", [1, 80], F32)
    alrow = p.sb("alrow3", [1, 80], F32)
    a4row = p.sb("a4row3", [1, 80], F32)
    a3row = p.sb("a3row3", [1, 80], F32)
    phg = p.sb("phg3", [128, 80], F32)
    alg4 = p.sb("alg43", [128, 80], F32)
    alg3 = p.sb("alg33", [128, 80], F32)
    scl5 = p.sb("scl53", [128, 5], F32)
    onesr = p.sb("onesr3b", [1, 128], F32)
    ones64 = p.sb("ones643", [1, 64], F32)
    onescb = p.sb("onescb3", [128, 1], BF16)
    cv64 = p.sb("cv643", [64, 1], F32)
    rrs = p.sb("rrs3", [1, 512], F32)
    rrsb = p.sb("rrsb3", [64, 512], F32)
    nbuf = p.sb("nbuf3", [64, 512], F32)
    oev = [p.sb(f"oev3{i}", [64, 512], BF16) for i in range(3)]

    psT = [p.ps(f"psT3{i}", [128, 512], F32) for i in range(2)]
    pacc = [p.ps(f"pacc3{i}", [128, 512], F32) for i in range(4)]
    pav = p.ps("pav3", [64, 512], F32)
    psm = p.ps("psm3", [1, 512], F32)

    for nm in ["vs", "ss", "ts"]:
        p.sem(nm)
    p.dma_pool("lk", 4)
    p.dma_pool("lv", 4)
    p.dma_pool("so", 4)

    cl = [p.dma("sync", "lv", lambda q: q.dma_start(out=phirow[:], in_=phi_i[:, :])),
          p.dma("sync", "lv", lambda q: q.dma_start(out=alrow[:], in_=alpha_i[:, :])),
          p.dma("gpsimd", "lv", lambda q: q.dma_start(out=diag[:], in_=ident_i[:, :]))]

    def v_pre(q):
        for s_ in cl:
            q.wait_ge(p.h(s_[0]), s_[1])
        q.memset(onesr[:], 1.0)
        q.memset(ones64[:], 1.0)
        q.memset(onescb[:], 1.0)
        for f in range(5):
            q.memset(scl5[:, f:f + 1], TWO_PI / (3.0 * FREQS[f]))
        q.tensor_scalar(out=a4row[:], in0=phirow[:], scalar1=1.0 / TWO_PI, scalar2=MAGIC,
                        op0=ALU.mult, op1=ALU.add)
        q.tensor_scalar(out=a4row[:], in0=a4row[:], scalar1=-MAGIC, scalar2=-TWO_PI,
                        op0=ALU.add, op1=ALU.mult)
        q.tensor_tensor(out=a4row[:], in0=phirow[:], in1=a4row[:], op=ALU.add)
        q.tensor_scalar(out=phirow[:], in0=a4row[:], scalar1=1.0 / 3.0, scalar2=None,
                        op0=ALU.mult)
        q.tensor_scalar(out=a4row[:], in0=alrow[:], scalar1=1.0, scalar2=None, op0=ALU.mult)
        return q.tensor_scalar(out=a3row[:], in0=alrow[:], scalar1=1.0, scalar2=None, op0=ALU.mult)
    vs_pre = p.step("vector", "vs", v_pre)

    def t_bg(q, w=vs_pre):
        q.wait_ge(p.h("vs"), w)
        return q.matmul(pacc[0][:, 0:80], onesr[:], phirow[:], start=True, stop=True)
    ts_bg = p.step("tensor", "ts", t_bg)

    def v_bg(q, w=ts_bg):
        q.wait_ge(p.h("ts"), w)
        return q.tensor_copy(out=phg[:], in_=pacc[0][:, 0:80])
    vs_bg = p.step("vector", "vs", v_bg)

    def t_bg2(q, w=vs_bg):
        q.wait_ge(p.h("vs"), w)
        return q.matmul(pacc[0][:, 0:80], onesr[:], a4row[:], start=True, stop=True)
    ts_bg2 = p.step("tensor", "ts", t_bg2)

    def v_bg2(q, w=ts_bg2):
        q.wait_ge(p.h("ts"), w)
        return q.tensor_copy(out=alg4[:], in_=pacc[0][:, 0:80])
    vs_bg2 = p.step("vector", "vs", v_bg2)

    def t_bg3(q, w=vs_bg2):
        q.wait_ge(p.h("vs"), w)
        return q.matmul(pacc[0][:, 0:80], onesr[:], a3row[:], start=True, stop=True)
    ts_bg3 = p.step("tensor", "ts", t_bg3)

    def v_bg3(q, w=ts_bg3):
        q.wait_ge(p.h("ts"), w)
        return q.tensor_copy(out=alg3[:], in_=pacc[0][:, 0:80])
    vs_bg3 = p.step("vector", "vs", v_bg3)

    head_end = [0, 0]
    psT_guard = [0, 0]
    wt_guard = [0, 0]
    sacc_guard = [0]
    sf_guard = [0] * 5
    mf_guard = [0] * 3
    uw_guard = 0
    pav_guard = ("vs", vs_bg3)
    cv_guard = 0
    ev_i = 0
    ostore_prev = {}

    for h in range(NH):
        b = h % 2
        lds = []
        for i in range(8):
            def fk(q, h=h, b=b, i=i, w=head_end[b]):
                if w and i == 0:
                    q.wait_ge(p.h("ts"), w)
                return q.dma_start(out=Kt[b][:, i * 512:(i + 1) * 512],
                                   in_=KT_i[h * 64:(h + 1) * 64, i * 512:(i + 1) * 512])
            lds.append(p.dma("gpsimd", "lk", fk))
        for i in range(8):
            def fq_(q, h=h, b=b, i=i):
                return q.dma_start(out=Qt[b][:, i * 512:(i + 1) * 512],
                                   in_=QT_i[h * 64:(h + 1) * 64, i * 512:(i + 1) * 512])
            lds.append(p.dma("gpsimd", "lk", fq_))
        for jt in range(32):
            def fv(q, h=h, b=b, jt=jt, w=head_end[b]):
                if w and jt == 0:
                    q.wait_ge(p.h("ts"), w)
                return q.dma_start(out=Vh[b][:, jt * HD:(jt + 1) * HD],
                                   in_=V_i[jt * 128:(jt + 1) * 128, h * 64:(h + 1) * 64])
            lds.append(p.dma("sync", "lv", fv))

        def t_cv(q, b=b, lds=lds, wcv=cv_guard):
            if wcv:
                q.wait_ge(p.h("vs"), wcv)
            for s_ in lds:
                q.wait_ge(p.h(s_[0]), s_[1])
            for jt in range(32):
                last = q.matmul(pav[0:64, 0:1], Vh[b][:, jt * HD:(jt + 1) * HD], onescb[:],
                                start=(jt == 0), stop=(jt == 31))
            return last
        ts_cv = p.step("tensor", "ts", t_cv)

        def v_cv(q, w=ts_cv):
            q.wait_ge(p.h("ts"), w)
            return q.tensor_copy(out=cv64[:], in_=pav[0:64, 0:1])
        cv_guard = p.step("vector", "vs", v_cv)

        last_av = 0
        for qc in range(8):
            for jg in range(8):
                mm_ms = []
                for i in range(4):
                    jt = jg * 4 + i
                    bj = i % 2

                    def t_s(q, b=b, qc=qc, jt=jt, bj=bj, wfree=psT_guard[bj]):
                        if wfree:
                            q.wait_ge(p.h("vs"), wfree)
                        return q.matmul(psT[bj][:], Kt[b][:, jt * 128:(jt + 1) * 128],
                                        Qt[b][:, qc * 512:(qc + 1) * 512], start=True, stop=True)
                    ts_s = p.step("tensor", "ts", t_s)

                    def v_ev(q, i=i, bj=bj, w=ts_s, wuw=uw_guard if i == 0 else 0):
                        q.wait_ge(p.h("ts"), w)
                        if wuw:
                            q.wait_ge(p.h("ss"), wuw)
                        return q.tensor_copy(out=uw[:, i * 512:(i + 1) * 512], in_=psT[bj][:])
                    vs_ev = p.step("vector", "vs", v_ev)
                    psT_guard[bj] = vs_ev

                mm_prev = 0
                for f in range(5):
                    fb = f
                    mb = f % 3
                    col = h * 5 + f

                    def s_trig(q, f=f, fb=fb, col=col, wsf=sf_guard[fb], wev=psT_guard[1]):
                        if f == 0:
                            q.wait_ge(p.h("vs"), wev)
                        if wsf:
                            q.wait_ge(p.h("vs"), wsf)
                        q.activation(out=sf[fb][:], in_=uw[:], func=AF.Sin,
                                     bias=phg[:, col:col + 1], scale=scl5[:, f:f + 1])
                        return q.activation(out=s2t[fb][:], in_=sf[fb][:], func=AF.Square,
                                            bias=0.0, scale=1.0)
                    ss_trig = p.step("scalar", "ss", s_trig)
                    uw_guard = ss_trig

                    def v_poly(q, f=f, fb=fb, mb=mb, col=col, w=ss_trig, wmf=mf_guard[mb]):
                        q.wait_ge(p.h("ss"), w)
                        if wmf:
                            q.wait_ge(p.h("ts"), wmf)
                        q.tensor_scalar(out=pt[:], in0=s2t[fb][:], scalar1=-4.0,
                                        scalar2=3.0, op0=ALU.mult, op1=ALU.add)
                        return q.scalar_tensor_tensor(out=mfb[mb][:], in0=sf[fb][:],
                                                      scalar=alg4[:, col:col + 1],
                                                      in1=pt[:], op0=ALU.mult, op1=ALU.mult)
                    vpoly = p.step("vector", "vs", v_poly)
                    sf_guard[fb] = vpoly

                    def t_mf(q, f=f, mb=mb, w=vpoly, wtn=sacc_guard[0] if f == 0 else 0):
                        q.wait_ge(p.h("vs"), w)
                        if f == 0 and wtn:
                            q.wait_ge(p.h("ss"), wtn)
                        for ch in range(4):
                            last = q.matmul(pacc[ch][:], diag[:],
                                            mfb[mb][:, ch * 512:(ch + 1) * 512],
                                            start=(f == 0), stop=(f == 4))
                        return last
                    mm_prev = p.step("tensor", "ts", t_mf)
                    mf_guard[mb] = mm_prev

                wb = jg % 2

                def s_tanh(q, wb=wb, w=mm_prev, wwt=wt_guard[wb]):
                    q.wait_ge(p.h("ts"), w)
                    if wwt:
                        q.wait_ge(p.h("ts"), wwt)
                    for ch in range(4):
                        last = q.activation(out=WT3[wb][:, ch * 512:(ch + 1) * 512],
                                            in_=pacc[ch][:], func=AF.Tanh, bias=0.0, scale=0.5)
                    return last
                ss_tanh = p.step("scalar", "ss", s_tanh)
                sacc_guard[0] = ss_tanh

                def t_av(q, b=b, jg=jg, wb=wb, w=ss_tanh, wpav=pav_guard):
                    q.wait_ge(p.h("ss"), w)
                    if jg == 0 and wpav[1]:
                        q.wait_ge(p.h(wpav[0]), wpav[1])
                    for i in range(4):
                        jt = jg * 4 + i
                        q.matmul(pav[:], Vh[b][:, jt * HD:(jt + 1) * HD],
                                 WT3[wb][:, i * 512:(i + 1) * 512],
                                 start=(jg == 0 and i == 0), stop=(jg == 7 and i == 3))
                    for i in range(4):
                        last = q.matmul(psm[:], onescb[:], WT3[wb][:, i * 512:(i + 1) * 512],
                                        start=(jg == 0 and i == 0), stop=(jg == 7 and i == 3))
                    return last
                ts_av = p.step("tensor", "ts", t_av)
                wt_guard[wb] = ts_av
                last_av = ts_av

            def v_f1(q, w=last_av):
                q.wait_ge(p.h("ts"), w)
                q.tensor_scalar(out=rrs[:], in0=psm[:], scalar1=4096.0, scalar2=None, op0=ALU.add)
                return q.reciprocal(out=rrs[:], in_=rrs[:])
            vs_f1 = p.step("vector", "vs", v_f1)

            def t_rb(q, w=vs_f1, wev=psT_guard[0]):
                q.wait_ge(p.h("vs"), max(w, wev))
                return q.matmul(psT[0][0:64, :], ones64[:], rrs[:], start=True, stop=True)
            ts_rb = p.step("tensor", "ts", t_rb)

            ob = ev_i % 3
            ev_i += 1

            def v_f2(q, ob=ob, w=ts_rb, wso=ostore_prev.get(ob)):
                q.wait_ge(p.h("ts"), w)
                if wso is not None:
                    q.wait_ge(p.h(wso[0]), wso[1])
                q.tensor_copy(out=rrsb[:], in_=psT[0][0:64, :])
                q.tensor_scalar(out=nbuf[:], in0=pav[:], scalar1=cv64[:, 0:1], scalar2=None,
                                op0=ALU.add)
                return q.tensor_tensor(out=oev[ob][:], in0=nbuf[:], in1=rrsb[:], op=ALU.mult)
            vs_f2 = p.step("vector", "vs", v_f2)
            pav_guard = ("vs", vs_f2)
            psT_guard[0] = vs_f2

            def st_o(q, h=h, qc=qc, ob=ob, w=vs_f2):
                q.wait_ge(p.h("vs"), w)
                return q.dma_start(out=yT_o[h * 64:(h + 1) * 64, qc * 512:(qc + 1) * 512],
                                   in_=oev[ob][:])
            ostore_prev[ob] = p.dma("sync", "so", st_o)
        head_end[b] = last_av

    so_t = p.pool_totals("so")
    p.on("gpsimd", lambda q: p.emit_pool_wait(q, so_t))
    nc = p.build()
    return p, nc









def build_F3c(core_id):
    p = Prog()
    yT_i = p.din("yT3", [D, T], BF16)
    WoT_i = p.din("WoT", [D, D], F32)
    bo_i = p.din("bog", [128, 8], F32)
    gam_i = p.din("gamg", [128, 8], F32)
    bet_i = p.din("betg", [128, 8], F32)
    x1T_i = p.din("x1T", [D, T], F32)
    bT_i = p.din("bT", [D, T], F32)
    ident_i = p.din("ident", [128, 128], F32)
    out_o = p.dout("out_tok", [T, D], F32)

    wstp = p.sb("wstpo", [128, 8192], BF16)
    bog = p.sb("bogo", [128, 8], F32)
    gamg = p.sb("gamgo", [128, 8], F32)
    betg = p.sb("betgo", [128, 8], F32)
    ident = p.sb("idento", [128, 128], F32)
    aow = p.sb("aowo", [128, 4096], F32)
    sq = [p.sb(f"sqo{i}", [128, 512], F32) for i in range(2)]
    murow = p.sb("murowo", [1, 512], F32)
    m2row = p.sb("m2rowo", [1, 512], F32)
    varrow = p.sb("varrowo", [1, 512], F32)
    rsrow = p.sb("rsrowo", [1, 512], F32)
    mub = p.sb("mubo", [128, 512], F32)
    rsb = p.sb("rsbo", [128, 512], F32)
    t1b = p.sb("t1bo", [128, 512], F32)
    xadd = [p.sb(f"xaddo{i}", [128, 512], F32) for i in range(2)]
    btst = [p.sb(f"btsto{i}", [128, 512], F32) for i in range(2)]
    fw = p.sb("fwo", [128, 512], F32)
    otile = [p.sb(f"otileo{i}", [128, 1024], F32) for i in range(4)]
    consts = p.sb("constso", [1, 4], F32)
    onesr = p.sb("onesro", [1, 128], F32)
    onescf = p.sb("onescfo", [128, 1], F32)

    pa = p.ps("pao", [128, 512], F32)
    ps_s = p.ps("psso", [1, 512], F32)
    ps_q = p.ps("psqo", [1, 512], F32)
    pbc = p.ps("pbco", [128, 512], F32)
    ptr = p.ps("ptro", [128, 128], F32)

    for nm in ["vs", "ss", "ts"]:
        p.sem(nm)
    p.dma_pool("lx", 4)
    p.dma_pool("lw", 4)
    p.dma_pool("so", 4)

    rs_y = RhsStream(p, "rsyo", [128, 512], BF16, n=10, pool="lx", gate_sem="ts")

    cl = []
    for cc in range(8):
        for oc in range(8):
            def f(q, cc=cc, oc=oc):
                return q.dma_start(out=wstp[:, oc * 1024 + cc * 128:oc * 1024 + (cc + 1) * 128],
                                   in_=WoT_i[cc * 128:(cc + 1) * 128, oc * 128:(oc + 1) * 128])
            cl.append(p.dma("gpsimd", "lw", f))
    for t_, s_ in [(bog, bo_i), (gamg, gam_i), (betg, bet_i), (ident, ident_i)]:
        cl.append(p.dma("sync", "lw", lambda q, t_=t_, s_=s_: q.dma_start(out=t_[:], in_=s_[:, :])))

    def vinit(q):
        q.memset(onesr[:], 1.0)
        q.memset(onescf[:], 1.0)
        q.memset(consts[:, 0:1], 1e-5)
        q.memset(consts[:, 1:2], -0.5)
        for s_ in cl:
            q.wait_ge(p.h(s_[0]), s_[1])
        return q.memset(consts[:, 2:3], 0.0)
    vs_init = p.step("vector", "vs", vinit)

    aw_free = [0] * 8
    sq_free = [0, 0]
    row_read = 0
    bc_read = vs_init
    ot_free = [None] * 4
    tr_read = 0
    xg_free = [0, 0]

    for ncc in range(8):
        ts_sq = None
        for oc in range(8):
            lds = [rs_y.load(yT_i[cc * 128:(cc + 1) * 128, ncc * 512:(ncc + 1) * 512])
                   for cc in range(8)]

            def t_mm(q, lds=lds, oc=oc, first=(ncc == 0 and oc == 0), vi=vs_init):
                if first:
                    q.wait_ge(p.h("vs"), vi)
                for cc in range(8):
                    rs_y.wait(q, lds[cc])
                    last = q.matmul(pa[:], wstp[:, oc * 1024 + cc * 128:oc * 1024 + (cc + 1) * 128],
                                    rs_y.ap(lds[cc]), start=(cc == 0), stop=(cc == 7))
                return last
            ts_mm = p.step("tensor", "ts", t_mm)
            rs_y.set_free(lds, ts_mm)

            def v_ev(q, oc=oc, w=ts_mm, wfree=aw_free[oc]):
                q.wait_ge(p.h("ts"), w)
                if wfree:
                    q.wait_ge(p.h("vs"), wfree)
                return q.tensor_scalar(out=aow[:, oc * 512:(oc + 1) * 512], in0=pa[:],
                                       scalar1=bog[:, oc:oc + 1], scalar2=None, op0=ALU.add)
            vs_ev = p.step("vector", "vs", v_ev)

            b = oc % 2

            def s_sq(q, oc=oc, b=b, w=vs_ev, wfree=sq_free[b]):
                q.wait_ge(p.h("vs"), w)
                if wfree:
                    q.wait_ge(p.h("ts"), wfree)
                return q.activation(out=sq[b][:], in_=aow[:, oc * 512:(oc + 1) * 512],
                                    func=AF.Square, bias=0.0, scale=1.0)
            ss_sq = p.step("scalar", "ss", s_sq)

            def t_sq(q, oc=oc, b=b, w=ss_sq, wrr=row_read, wev=vs_ev):
                q.wait_ge(p.h("ss"), w)
                if oc == 0 and wrr:
                    q.wait_ge(p.h("vs"), wrr)
                q.wait_ge(p.h("vs"), wev)
                q.matmul(ps_s[:], onescf[:], aow[:, oc * 512:(oc + 1) * 512],
                         start=(oc == 0), stop=(oc == 7))
                return q.matmul(ps_q[:], onescf[:], sq[b][:], start=(oc == 0), stop=(oc == 7))
            ts_sq = p.step("tensor", "ts", t_sq)
            sq_free[b] = ts_sq

        def v_row(q, w=ts_sq):
            q.wait_ge(p.h("ts"), w)
            q.tensor_scalar(out=murow[:], in0=ps_s[:], scalar1=1.0 / 1024.0, scalar2=None,
                            op0=ALU.mult)
            q.tensor_tensor(out=m2row[:], in0=murow[:], in1=murow[:], op=ALU.mult)
            return q.scalar_tensor_tensor(out=varrow[:], in0=ps_q[:], scalar=1.0 / 1024.0,
                                          in1=m2row[:], op0=ALU.mult, op1=ALU.subtract)
        vs_row = p.step("vector", "vs", v_row)
        row_read = vs_row

        def s_rstd(q, w=vs_row):
            q.wait_ge(p.h("vs"), w)
            q.activation(out=varrow[:], in_=varrow[:], func=AF.Ln,
                         bias=consts[0:1, 0:1], scale=1.0)
            return q.activation(out=rsrow[:], in_=varrow[:], func=AF.Exp, bias=0.0,
                                scale=consts[0:1, 1:2])
        ss_rstd = p.step("scalar", "ss", s_rstd)

        def t_bc(q, w=ss_rstd, wbc=bc_read):
            q.wait_ge(p.h("ss"), w)
            q.wait_ge(p.h("vs"), wbc)
            q.matmul(pbc[:], onesr[:], rsrow[:], start=True, stop=True)
            return q.matmul(pa[:], onesr[:], murow[:], start=True, stop=True)
        ts_bc = p.step("tensor", "ts", t_bc)

        def v_bc(q, w=ts_bc):
            q.wait_ge(p.h("ts"), w)
            q.tensor_copy(out=rsb[:], in_=pbc[:])
            return q.tensor_copy(out=mub[:], in_=pa[:])
        vs_bc = p.step("vector", "vs", v_bc)
        bc_read = vs_bc

        for oc in range(8):
            b = oc % 2
            lx1 = p.dma("sync", "lx", lambda q, oc=oc, ncc=ncc, b=b, w=xg_free[b]:
                        ((q.wait_ge(p.h("vs"), w) if w else None),
                         q.dma_start(out=xadd[b][:],
                                     in_=x1T_i[oc * 128:(oc + 1) * 128,
                                               ncc * 512:(ncc + 1) * 512]))[-1])
            lbt = p.dma("sync", "lx", lambda q, oc=oc, ncc=ncc, b=b, w=xg_free[b]:
                        ((q.wait_ge(p.h("vs"), w) if w else None),
                         q.dma_start(out=btst[b][:],
                                    in_=bT_i[oc * 128:(oc + 1) * 128,
                                             ncc * 512:(ncc + 1) * 512]))[-1])

            def v_fin(q, oc=oc, b=b, w=vs_bc, lx1=lx1, lbt=lbt, wtr=tr_read):
                q.wait_ge(p.h("vs"), max(w, wtr))
                q.wait_ge(p.h(lx1[0]), lx1[1])
                q.wait_ge(p.h(lbt[0]), lbt[1])
                q.tensor_sub(out=t1b[:], in0=aow[:, oc * 512:(oc + 1) * 512], in1=mub[:])
                q.tensor_tensor(out=t1b[:], in0=t1b[:], in1=rsb[:], op=ALU.mult)
                q.tensor_scalar(out=t1b[:], in0=t1b[:], scalar1=gamg[:, oc:oc + 1],
                                scalar2=betg[:, oc:oc + 1], op0=ALU.mult, op1=ALU.add)
                q.tensor_add(out=t1b[:], in0=t1b[:], in1=xadd[b][:])
                return q.tensor_add(out=fw[:], in0=t1b[:], in1=btst[b][:])
            vs_fin = p.step("vector", "vs", v_fin)
            aw_free[oc] = vs_fin
            xg_free[b] = vs_fin

            vtr = 0
            for tl in range(4):
                def t_tr(q, tl=tl, w=vs_fin, wv=tr_read):
                    q.wait_ge(p.h("vs"), max(w, wv))
                    return q.transpose(ptr[:], fw[:, tl * 128:(tl + 1) * 128], ident[:])
                ts_tr = p.step("tensor", "ts", t_tr)

                def v_tr(q, oc=oc, tl=tl, w=ts_tr, wst=ot_free[tl] if oc == 0 else None):
                    q.wait_ge(p.h("ts"), w)
                    if wst is not None:
                        q.wait_ge(p.h(wst[0]), wst[1])
                    return q.tensor_copy(out=otile[tl][:, oc * 128:(oc + 1) * 128], in_=ptr[:])
                vtr = p.step("vector", "vs", v_tr)
                tr_read = vtr

        for tl in range(4):
            def st_o(q, ncc=ncc, tl=tl, w=tr_read):
                q.wait_ge(p.h("vs"), w)
                return q.dma_start(out=out_o[ncc * 512 + tl * 128:ncc * 512 + (tl + 1) * 128, :],
                                   in_=otile[tl][:, :])
            ot_free[tl] = p.dma("sync", "so", st_o)

    so_t = p.pool_totals("so")
    p.on("gpsimd", lambda q: p.emit_pool_wait(q, so_t))
    nc = p.build()
    return p, nc




def host_masks():
    mq = np.zeros((4, 128, 512), np.float32)
    mt = np.zeros((4, 128, 512), np.float32)
    for a in range(4):
        q_loc = a * 128 + np.arange(128)[:, None]
        j_loc = np.arange(512)[None, :]
        mq[a][j_loc > q_loc] = -30000.0
        j_loc2 = a * 128 + np.arange(128)[:, None]
        q_loc2 = np.arange(512)[None, :]
        mt[a][j_loc2 > q_loc2] = -30000.0
    return mq, mt


def host_rope32():
    inv_freq = 1.0 / 10000.0 ** (np.arange(0, 32, 2, dtype=np.float32) / 32)
    tt = np.arange(T, dtype=np.float32)
    freqs = np.outer(tt, inv_freq)
    emb = np.concatenate([freqs, freqs], -1)
    return (np.ascontiguousarray(np.cos(emb).T.astype(np.float32)),
            np.ascontiguousarray(np.sin(emb).T.astype(np.float32)))


def _f32(a):
    return np.ascontiguousarray(np.asarray(a, dtype=np.float32))


def kernel(**z):
    cores = list(range(NCORES))
    x = _f32(z["L0_attn_x"])
    x_all = np.ascontiguousarray(x.reshape(NTOK, D))
    xt = [np.ascontiguousarray(x[c].T) for c in cores]
    lt, ident, iota = host_consts()
    mq, mt = host_masks()
    cosT32, sinT32 = host_rope32()
    cos64, sin64 = host_rope64()

    def run(nc, in_maps, ids):
        res = run_bass_kernel_spmd(nc, in_maps, core_ids=ids)
        return res.results

    _, nc = build_A(0)
    A0 = run(nc, [{"x_all": x_all, "gwT": _f32(z["L0_ffn_gate_weight"].T),
                   "lt_in": lt, "id_in": ident, "io_in": iota}], [0])[0]

    _, nc = build_B()
    B0 = run(nc, [{"xn_tm": np.asarray(A0["xn_tm"]), "slotu": np.asarray(A0["slotu"])}], [0])[0]

    _, nc = build_D()
    D0 = run(nc, [{"buf": np.asarray(B0["buf"]),
                   "gupT": _f32(np.transpose(z["L0_ffn_gate_up_proj"], (0, 2, 1))),
                   "dnT": _f32(np.transpose(z["L0_ffn_down_proj"], (0, 2, 1)))}], [0])[0]

    slotu = np.asarray(A0["slotu"])
    wslot = np.asarray(A0["wslot"])
    stats = np.asarray(A0["stats"])
    eo = np.asarray(D0["eo"])

    _, nc = build_E1(0)
    rE1 = run(nc, [{"eo": eo, "slotu": np.ascontiguousarray(slotu[c * T:(c + 1) * T]),
                    "wslot": np.ascontiguousarray(wslot[c * T:(c + 1) * T])} for c in cores], cores)
    ffn0 = [np.asarray(r["ffn0"]) for r in rE1]

    _, nc = build_E2a(0)
    wkv = _f32(z["L0_attn_W_kv"]); gkv = _f32(z["L0_attn_g_kv"]).reshape(KVR, 1)
    rE2a = run(nc, [{"xt_own": xt[c], "stats": np.ascontiguousarray(stats[:, c * T:(c + 1) * T]),
                     "Wkv": wkv, "gkv": gkv,
                     "Wkdec": _f32(z["L0_attn_W_kdec"]), "Wvdec": _f32(z["L0_attn_W_vdec"]),
                     "Wq": _f32(z["L0_attn_W_q"]), "Wqdec": _f32(z["L0_attn_W_qdec"]),
                     "Wkrope": _f32(z["L0_attn_W_krope"]), "Wqrope": _f32(z["L0_attn_W_qrope"]),
                     "cosT": cosT32, "sinT": sinT32} for c in cores], cores)

    _, nc = build_E2b(0)
    rE2b = run(nc, [{"KT": np.asarray(rE2a[c]["KT"]), "QT": np.asarray(rE2a[c]["QT"]),
                     "V": np.asarray(rE2a[c]["V"]), "maskt": mt} for c in cores], cores)

    _, nc = build_E2c(0)
    wo0 = _f32(z["L0_attn_W_o"])
    rE2c = run(nc, [{"xt_own": xt[c], "OT": np.asarray(rE2b[c]["OT"]), "Wo": wo0,
                     "ffn0": ffn0[c], "ident": ident} for c in cores], cores)
    x1T = [np.asarray(r["x1T"]) for r in rE2c]

    _, nc = build_F0(0)
    rF0 = run(nc, [{"x1T": x1T[c]} for c in cores], cores)

    _, nc = build_F1a(0)
    qkvT = _f32(z["L1_ffn_qkv_w"].T)
    w1c = _f32(np.asarray(z["L1_ffn_norm1_w"]).reshape(8, 128).T)
    rF1a = run(nc, [{"xn1Tf": np.asarray(rF0[c]["xn1Tf"]), "qkvT": qkvT, "w1c": w1c,
                     "cosT64": cos64, "sinT64": sin64} for c in cores], cores)

    _, nc = build_F1b(0)
    rF1b = run(nc, [{"QrT": np.asarray(rF1a[c]["QrT"]), "KrT": np.asarray(rF1a[c]["KrT"]),
                     "V": np.asarray(rF1a[c]["V"]), "maskt": mt} for c in cores], cores)

    _, nc = build_F1c(0)
    projT = _f32(z["L1_ffn_proj_w"].T)
    w2c = _f32(np.asarray(z["L1_ffn_norm2_w"]).reshape(8, 128).T)
    routT = _f32(z["L1_ffn_router_w"].T)
    rF1c = run(nc, [{"xn1Tf": np.asarray(rF0[c]["xn1Tf"]), "yT": np.asarray(rF1b[c]["yT"]),
                     "projT": projT, "w2c": w2c, "routT": routT, "ident": ident}
                    for c in cores], cores)

    wgT = _f32(np.transpose(z["L1_ffn_wg"], (0, 2, 1)))
    wuT = _f32(np.transpose(z["L1_ffn_wu"], (0, 2, 1)))
    wdT = _f32(np.transpose(z["L1_ffn_wd"], (0, 2, 1)))
    _, nc = build_F2b(0, 0, 4, "a")
    rM1 = run(nc, [{"xmT": np.asarray(rF1c[c]["xmT"]), "gateT": np.asarray(rF1c[c]["gateT"]),
                    "macc_in": np.asarray(rF1c[c]["aT"]), "wgT": wgT, "wuT": wuT, "wdT": wdT}
                   for c in cores], cores)
    _, nc = build_F2b(0, 4, 8, "b")
    rM2 = run(nc, [{"xmT": np.asarray(rF1c[c]["xmT"]), "gateT": np.asarray(rF1c[c]["gateT"]),
                    "macc_in": np.asarray(rM1[c]["macc_out"]), "wgT": wgT, "wuT": wuT, "wdT": wdT}
                   for c in cores], cores)
    bT = [np.asarray(r["macc_out"]) for r in rM2]

    _, nc = build_F3a(0)
    WqT = _f32(z["L1_attn_Wq"].T); WkT = _f32(z["L1_attn_Wk"].T); WvT = _f32(z["L1_attn_Wv"].T)
    bqg = _f32(np.asarray(z["L1_attn_bq"]).reshape(8, 128).T)
    bkg = _f32(np.asarray(z["L1_attn_bk"]).reshape(8, 128).T)
    bvr = _f32(np.asarray(z["L1_attn_bv"]).reshape(1, D))
    tmp = _f32(np.asarray(z["L1_attn_temperature"]).reshape(1, 1))
    rF3a = run(nc, [{"xn1Tb": np.asarray(rF0[c]["xn1Tb"]), "WqT": WqT, "WkT": WkT, "WvT": WvT,
                     "bqg": bqg, "bkg": bkg, "bvr": bvr, "temp": tmp} for c in cores], cores)

    _, nc = build_F3b(0)
    phir = _f32(np.asarray(z["L1_attn_phi"]).reshape(1, 80))
    alphar = _f32(np.asarray(z["L1_attn_alpha"]).reshape(1, 80))
    rF3b = run(nc, [{"QT3": np.asarray(rF3a[c]["QT3"]), "KT3": np.asarray(rF3a[c]["KT3"]),
                     "V3": np.asarray(rF3a[c]["V3"]), "phir": phir, "alphar": alphar,
                     "ident": ident} for c in cores], cores)

    _, nc = build_F3c(0)
    WoT = _f32(z["L1_attn_Wo"].T)
    bog = _f32(np.asarray(z["L1_attn_bo"]).reshape(8, 128).T)
    gamg = _f32(np.asarray(z["L1_attn_gamma"]).reshape(8, 128).T)
    betg = _f32(np.asarray(z["L1_attn_beta"]).reshape(8, 128).T)
    rF3c = run(nc, [{"yT3": np.asarray(rF3b[c]["yT3"]), "WoT": WoT, "bog": bog, "gamg": gamg,
                     "betg": betg, "x1T": x1T[c], "bT": bT[c], "ident": ident}
                    for c in cores], cores)

    out = np.stack([np.asarray(rF3c[c]["out_tok"], dtype=np.float32) for c in cores], 0)
    return out
